# version 03
# baseline (speedup 1.0000x reference)
import numpy as np
import ml_dtypes
import concourse.bass as bass
import concourse.mybir as mybir
from concourse.bass_utils import run_bass_kernel_spmd
from concourse.bass_types import DynSlice
from contextlib import ExitStack

F32 = mybir.dt.float32
BF16 = mybir.dt.bfloat16
AF = mybir.ActivationFunctionType
ALU = mybir.AluOpType
AX = mybir.AxisListType
bfdt = ml_dtypes.bfloat16

SEQ, D = 4096, 4096
NH, HP, NG, CH = 64, 64, 4, 128
NCH = SEQ // CH
TOK3 = 2048
EPS = 1e-5
R_GATE, R_HS, R_B, R_C, R_A, R_DTWR, R_PAD = 0, 4096, 8192, 8704, 9216, 10240, 10368
WROWS = 10496
FW = 128
NSLAB = WROWS // FW
CONVCH, NCONVFB = 5120, 40
MAGIC = 987654.0
TSUP, NSUP, TTW, NTT = 2048, 2, 512, 4


def build_nc(dump=()):
    nc = bass.Bass("TRN2", target_bir_lowering=False)

    def inp(name, shape, dt=F32):
        return nc.dram_tensor(name, shape, dt, kind="ExternalInput").ap()

    x_in = inp("x", [SEQ, D])
    wtile = inp("wtile", [NSLAB, 128, 4096])
    convw = inp("convw", [CONVCH, 4])
    convb = inp("convb", [CONVCH, 1])
    dtbias = inp("dtbias", [NH, 1])
    alog = inp("alog", [NH, 1])
    dvec = inp("dvec", [1, NH])
    nw = inp("nw", [1, NH * HP])
    wotile = inp("wotile", [32, 64, 128, 128])
    bmat = inp("bmat", [1024, 4096])
    c_i128f = inp("c_i128f", [128, 128])
    c_i128b = inp("c_i128b", [128, 128], BF16)
    c_upper = inp("c_upper", [128, 128])
    c_delta = inp("c_delta", [4, 2048])
    c_e127 = inp("c_e127", [128, 1])
    c_sel = inp("c_sel", [64, 1024], BF16)

    out = nc.dram_tensor("out", [TOK3, D], F32, kind="ExternalOutput").ap()

    def scr(name, shape, dt=BF16, **kw):
        return nc.dram_tensor(name, shape, dt, kind="Internal", **kw).ap()

    wT = scr("wT", [NSLAB, 128, 4096])
    woT = scr("woT", [32, 64, 128, 128])
    bnat = scr("bnat", [1024, 4096])
    xn_tok = scr("xn_tok", [SEQ, D])
    sg_s = scr("sg_s", [4096, SEQ])
    hs_s = scr("hs_s", [4096, SEQ])
    bx_s = scr("bx_s", [512, SEQ])
    cx_s = scr("cx_s", [512, SEQ])
    dt32_s = scr("dt32_s", [NH, SEQ], F32)
    dtb_s = scr("dtb_s", [NH, SEQ])
    hgq = scr("hgq", [2048, TSUP])
    acsflat = scr("acsflat", [64, 128], F32)
    attn_s = scr("attn_s", [D, TOK3])
    ffn_s = scr("ffn_s", [TOK3, D])
    normed_loc = scr("normed_loc", [SEQ, 4096])
    normed_loc3 = scr("normed_loc3", [TOK3, 8192])
    hgq_loc = scr("hgq_loc", [1024, TSUP])
    x_loc = scr("x_loc", [TOK3, D], F32)
    normed_sh = scr("normed_sh", [SEQ, 8192], addr_space="Shared")
    flag_sh = scr("flag_sh", [4, 64], F32, addr_space="Shared")
    bxcx_sh = scr("bxcx_sh", [2048, SEQ], addr_space="Shared")

    dmap = {"xn_tok": xn_tok, "sg_s": sg_s, "hs_s": hs_s, "bx_s": bx_s,
            "cx_s": cx_s, "dt32_s": dt32_s, "dtb_s": dtb_s, "hgq": hgq,
            "attn_s": attn_s, "ffn_s": ffn_s, "normed_loc": normed_loc,
            "x_loc": x_loc, "hgq_loc": hgq_loc}
    dumps = [(nc.dram_tensor("dump_" + n, list(dmap[n].shape), dmap[n].dtype,
                             kind="ExternalOutput").ap(), dmap[n]) for n in dump]

    es = ExitStack()
    with es:
        def SB(stack):
            def alloc(name, shape, dt=F32):
                return stack.enter_context(nc.sbuf_tensor(name, shape, dt))
            return alloc

        sbp = SB(es)
        i128f = sbp("i128f", [128, 128])
        i128b = sbp("i128b", [128, 128], BF16)
        e127 = sbp("e127", [128, 1])
        selt = sbp("selt", [64, 1024], BF16)
        onesrow = sbp("onesrow", [1, 512])
        acol = sbp("acol", [NH, 1])
        dtbias_t = sbp("dtbias_t", [NH, 1])
        dvec_row = sbp("dvec_row", [1, NH])
        nwrow = sbp("nwrow", [1, NH * HP])
        cw_t = sbp("cw_t", [128, 4 * NCONVFB])
        cb_t = sbp("cb_t", [128, NCONVFB])
        halo = sbp("halo", [128, 3 * NCONVFB], BF16)
        magic_t = sbp("magic_t", [1, 64])
        zrow = sbp("zrow", [1, 64])
        zero64 = sbp("zero64", [NH, 128])
        flagrd = sbp("flagrd", [1, 64])
        epscol = sbp("epscol", [128, 1])
        onecol = sbp("onecol", [128, 1])

        PS = es.enter_context(nc.psum_tensor("PS", [128, 4096], F32))

        def bank(i, n=1):
            return PS[:, 512 * i:512 * (i + n)]

        class Sem:
            def __init__(self, name):
                self.h = es.enter_context(nc.semaphore(name))
                self.n = 0

        qg, qs, qa, qs2 = Sem("qg"), Sem("qs"), Sem("qa"), Sem("qs2")
        qsp = Sem("qsp")
        qsp2 = Sem("qsp2")
        vch, sch, pch, tch = Sem("vch"), Sem("sch"), Sem("pch"), Sem("tch")
        block = es.enter_context(nc.Block())

        def _mk(engsel, chain, selfsync):
            def op(emit, waits=(), inc=None, k=1):
                def body(e):
                    for wt in waits:
                        if wt is None:
                            continue
                    for wt in waits:
                        if wt is None:
                            continue
                        sem, n = wt
                        if n > 0:
                            e.wait_ge(sem.h, n)
                    ins = emit(e)
                    tgt = chain if inc is None else inc
                    if tgt is not None:
                        ins.then_inc(tgt.h, k)
                        tgt.n += k
                        if selfsync and tgt is chain:
                            e.wait_ge(chain.h, chain.n)
                engsel(body)
            return op

        V = _mk(block.vector, vch, True)
        S = _mk(block.scalar, sch, True)
        P = _mk(block.gpsimd, pch, True)
        G = _mk(block.gpsimd, qg, False)

        def mkdma(engsel, sem):
            def op(emit, waits=()):
                def body(e):
                    for wt in waits:
                        if wt is None:
                            continue
                    for wt in waits:
                        if wt is None:
                            continue
                        s2, n = wt
                        if n > 0:
                            e.wait_ge(s2.h, n)
                    emit(e).then_inc(sem.h, 16)
                    sem.n += 16
                engsel(body)
            return op

        GD = mkdma(block.gpsimd, qg)
        Y = mkdma(block.sync, qs)
        Y2 = mkdma(block.sync, qs2)
        SD = mkdma(block.scalar, qa)

        def T(emit, waits=()):
            def body(t):
                for wt in waits:
                    if wt is None:
                        continue
                    sem, n = wt
                    if n > 0:
                        t.wait_ge(sem.h, n)
                emit(t).then_inc(tch.h, 1)
                tch.n += 1
            block.tensor(body)

        def tok(sem):
            return (sem, sem.n)

        for dst, src in [(i128f, c_i128f), (i128b, c_i128b),
                         (e127, c_e127), (selt, c_sel), (acol, alog),
                         (dtbias_t, dtbias), (dvec_row, dvec), (nwrow, nw)]:
            GD(lambda g, dst=dst, src=src: g.dma_start(dst[:, :], src))
        for fb in range(NCONVFB):
            GD(lambda g, fb=fb: g.dma_start(cw_t[:, 4 * fb:4 * fb + 4],
                                            convw[128 * fb:128 * fb + 128, :]))
            GD(lambda g, fb=fb: g.dma_start(cb_t[:, fb:fb + 1],
                                            convb[128 * fb:128 * fb + 128, :]))
        SMALL = tok(qg)

        V(lambda v: v.memset(zrow[:, :], 0.0))
        V(lambda v: v.memset(magic_t[:, :], MAGIC))
        V(lambda v: v.memset(zero64[:, :], 0.0))
        V(lambda v: v.memset(halo[:, :], 0.0))
        V(lambda v: v.memset(onesrow[:, :], 1.0))
        V(lambda v: v.memset(epscol[:, :], EPS))
        V(lambda v: v.memset(onecol[:, :], 1.0))
        CONSTS_V = tok(vch)

        parity = {}

        def _par(engname, engsel, et):
            def body(e):
                r = nc.alloc_register(et, engname + "_par")
                e.load(r, nc.partition_id_tensor.ap())
                e.reg_alu(r, r, 1, ALU.bitwise_and)
                parity[engname] = e.compute_val(r)
            engsel(body)

        _par("g", block.gpsimd, mybir.EngineType.Pool)
        _par("s", block.sync, mybir.EngineType.SP)

        def par(e, name):
            return e.snap(parity[name], min_val=0, max_val=1)

        Y(lambda s: s.dma_start(
            flag_sh[DynSlice(par(s, "s"), 1), :], zrow[:, :]),
          waits=[CONSTS_V])
        Y(lambda s: s.dma_start(
            flag_sh[DynSlice(par(s, "s") + 2, 1), :], zrow[:, :]))
        FLAG_CLEARED = tok(qs)

        MAGIC_U32 = int(np.float32(MAGIC).view(np.uint32))

        def emit_spin(rowbase, spinsem, tag):
            def body(s):
                with s.register(tag + "_flv") as flv, \
                     s.register(tag + "_oth") as oth, \
                     s.register(tag + "_cnt") as cnt:
                    s.load(oth, nc.partition_id_tensor.ap())
                    s.reg_alu(oth, oth, 1, ALU.bitwise_and)
                    s.reg_alu(oth, oth, 1, ALU.bitwise_xor)
                    s.reg_alu(oth, oth, rowbase, ALU.add)
                    o_v = s.snap(s.compute_val(oth), min_val=rowbase,
                                 max_val=rowbase + 1)
                    othf = flag_sh[DynSlice(o_v, 1), :]
                    s.reg_mov(cnt, 0)

                    def poll():
                        s.reg_add(cnt, cnt, 16)
                        s.dma_start(flagrd[:, :], othf).then_inc(spinsem.h, 16)
                        s.wait_ge(spinsem.h, s.compute_val(cnt))
                        s.load(flv, flagrd[0:1, 0:1].bitcast(mybir.dt.uint32))

                    poll()
                    with s.While(lambda: s.compute_val(flv) != MAGIC_U32):
                        poll()
            block.sync(body)

        for i in range(0, NSLAB, 6):
            GD(lambda g, i=i: g.dma_start(wT[i:min(i + 6, NSLAB), :, :],
                                          wtile[i:min(i + 6, NSLAB), :, :]))
        for i in range(32):
            GD(lambda g, i=i: g.dma_start(woT[i, :, :, :], wotile[i, :, :, :]))
        for i in range(2):
            GD(lambda g, i=i: g.dma_start(bnat[i * 512:(i + 1) * 512, :],
                                          bmat[i * 512:(i + 1) * 512, :]))
        CASTS = tok(qg)
        WT_DONE = CASTS
        WOT_DONE = CASTS

        S(lambda s: s.activation(acol[:, :], acol[:, :], AF.Exp), waits=[SMALL])
        S(lambda s: s.mul(acol[:, :], acol[:, :], -1.0))


        with ExitStack() as es1:
            sb1 = SB(es1)
            es1a = ExitStack()
            sb1a = SB(es1a)
            xsb = [sb1a(f"xs{i}", [128, D]) for i in range(2)]
            xnt = [es1a.enter_context(nc.sbuf_tensor(f"xnt{i}", [128, D], BF16))
                   for i in range(2)]
            lnstat = sb1a("lnstat", [128, 8])
            lntrash = es1a.enter_context(
                nc.sbuf_tensor("lntrash", [128, D], BF16))
            xn_done_v = []
            xn_spill = []
            QG_X0 = qg.n
            for i in range(32):
                w = []
                if i >= 2:
                    w.append((vch, xn_done_v[i - 2]))
                GD(lambda g, i=i: g.dma_start(xsb[i % 2][:, :],
                                              x_in[128 * i:128 * (i + 1), :]),
                   waits=w)
                XL = tok(qg)
                S(lambda s, i=i: s.activation(lntrash[:, :], xsb[i % 2][:, :],
                                              AF.Square,
                                              accum_out=lnstat[:, 1:2]),
                  waits=[XL])
                SQ = tok(sch)
                V(lambda v, i=i: v.reduce_sum(lnstat[:, 0:1], xsb[i % 2][:, :],
                                              axis=AX.X), waits=[XL])
                V(lambda v: v.tensor_scalar(lnstat[:, 2:3], lnstat[:, 0:1],
                                            1.0 / D, None, ALU.mult))
                V(lambda v: v.tensor_tensor(lnstat[:, 4:5], lnstat[:, 2:3],
                                            lnstat[:, 2:3], ALU.mult))
                V(lambda v: v.tensor_scalar(lnstat[:, 3:4], lnstat[:, 1:2],
                                            1.0 / D, lnstat[:, 4:5],
                                            ALU.mult, ALU.subtract), waits=[SQ])
                S(lambda s: s.activation(lnstat[:, 5:6], lnstat[:, 3:4], AF.Ln,
                                         bias=epscol[:, :]), waits=[tok(vch)])
                S(lambda s: s.activation(lnstat[:, 6:7], lnstat[:, 5:6], AF.Exp,
                                         scale=-0.5))
                w2 = [tok(sch)]
                if i >= 2:
                    w2.append((qg, xn_spill[i - 2]))
                V(lambda v, i=i: v.tensor_scalar(xnt[i % 2][:, :],
                                                 xsb[i % 2][:, :],
                                                 lnstat[:, 2:3], lnstat[:, 6:7],
                                                 ALU.subtract, ALU.mult),
                  waits=w2)
                xn_done_v.append(vch.n)
                GD(lambda g, i=i: g.dma_start(xn_tok[128 * i:128 * (i + 1), :],
                                              xnt[i % 2][:, :]),
                   waits=[(vch, xn_done_v[i])])
                xn_spill.append(qg.n)
            LN_DONE = tok(qg)
            es1a.close()

            xnT = [es1.enter_context(nc.sbuf_tensor(f"xnT{i}", [128, TSUP], BF16))
                   for i in range(32)]
            wslab = [es1.enter_context(nc.sbuf_tensor(f"wslab{i}",
                                                      [128, 32 * FW], BF16))
                     for i in range(2)]
            evac_t = [es1.enter_context(nc.sbuf_tensor(f"evac{i}", [128, TTW],
                                                       BF16)) for i in range(2)]
            convo_t = [es1.enter_context(nc.sbuf_tensor(f"convo{i}", [128, TTW],
                                                        BF16)) for i in range(2)]
            convin = es1.enter_context(nc.sbuf_tensor("convin", [128, TTW + 3],
                                                      BF16))
            diag_t = [es1.enter_context(nc.sbuf_tensor(f"diag{i}", [128, 128],
                                                       BF16)) for i in range(4)]
            dtw = [sb1(f"dtw{i}", [64, TTW]) for i in range(2)]
            scf = sb1("scf", [64, TTW])
            sct = sb1("sct", [128, 64])
            mx1, mx2 = sb1("mx1", [128, 8]), sb1("mx2", [128, 8])
            wk1, msk = sb1("wk1", [128, 64]), sb1("msk", [128, 64])
            gex, gsc = sb1("gex", [128, 64]), sb1("gsc", [128, 2])
            g_fm = es1.enter_context(nc.sbuf_tensor("g_fm", [64, TSUP], BF16))
            h_t = es1.enter_context(nc.sbuf_tensor("h_t", [128, TTW], BF16))
            hg_t = es1.enter_context(nc.sbuf_tensor("hg_t", [128, TTW], BF16))
            dt32t = sb1("dt32t", [64, TTW])
            dtbft = es1.enter_context(nc.sbuf_tensor("dtbft", [64, TTW], BF16))

            def fb_role(fb):
                r0 = fb * 128
                if r0 < R_HS:
                    return ("gate", r0 - R_GATE)
                if r0 < R_B:
                    return ("hs", r0 - R_HS)
                if r0 < R_C:
                    return ("b", r0 - R_B)
                if r0 < R_A:
                    return ("c", r0 - R_C)
                if r0 < R_DTWR:
                    return ("a", r0 - R_A)
                if r0 < R_PAD:
                    return ("dtwr", 0)
                return ("pad", 0)

            inproj_bank_free = [None] * 4
            conv_bank_free = [None] * 2
            evac_free = [None] * 2
            convo_free = [None] * 2
            conv_mm_last = [None]
            halo_upd_last = [None]
            hg_spill_last = [None]
            mm_group = 0
            conv_group = 0
            last_super_mm = 0

            for sup in range(NSUP):
                t0 = sup * TSUP
                XNT0 = qs.n
                for dd in range(32):
                    w = [LN_DONE] if dd == 0 else []
                    if sup > 0 and dd == 0:
                        w.append((tch, last_super_mm))
                    Y(lambda s, dd=dd, t0=t0: s.dma_start_transpose(
                        xnT[dd][:, :],
                        xn_tok[t0:t0 + TSUP, 128 * dd:128 * dd + 128]), waits=w)
                XNT_OK = (qs, XNT0 + 16 * 32)

                slab_order = list(range(72)) + [80] + list(range(72, 80))
                slab_loaded = {}
                slab_consumed = {}

                def load_slab(kk, sup=sup):
                    sl = slab_order[kk]
                    w = [WT_DONE] if (sup == 0 and kk == 0) else []
                    if sup > 0 and kk < 2:
                        w.append((tch, last_super_mm))
                    if kk >= 2:
                        w.append((tch, slab_consumed[kk - 2]))
                    GD(lambda g, sl=sl, kk=kk: g.dma_start(
                        wslab[kk % 2][:, :], wT[sl, :, :]), waits=w)
                    slab_loaded[kk] = qg.n

                load_slab(0)
                load_slab(1)

                for k, sl in enumerate(slab_order):
                    for blkk in [0]:
                        fb = sl
                        role, off = fb_role(fb)
                        if role == "pad":
                            continue
                        for tt in range(NTT):
                            gidx = mm_group
                            mm_group += 1
                            bnk = gidx % 4

                            def mm_emit(t, k=k, blkk=blkk, tt=tt, bnk=bnk):
                                for dd in range(32):
                                    ins = t.matmul(
                                        bank(bnk),
                                        wslab[k % 2][:, FW * dd + 128 * blkk:
                                                     FW * dd + 128 * blkk + 128],
                                        xnT[dd][:, TTW * tt:TTW * (tt + 1)],
                                        start=(dd == 0), stop=(dd == 31))
                                return ins

                            T(mm_emit, waits=[(qg, slab_loaded[k]), XNT_OK,
                                              inproj_bank_free[bnk]])
                            MM_OK = tok(tch)

                            if role == "gate":
                                ev = gidx % 2
                                S(lambda s, bnk=bnk, ev=ev: s.activation(
                                    evac_t[ev][:, :], bank(bnk), AF.Silu),
                                  waits=[MM_OK, evac_free[ev]])
                                inproj_bank_free[bnk] = tok(sch)
                                GD(lambda g, off=off, tt=tt, t0=t0, ev=ev:
                                   g.dma_start(
                                       sg_s[off:off + 128,
                                            t0 + TTW * tt:t0 + TTW * (tt + 1)],
                                       evac_t[ev][:, :]), waits=[tok(sch)])
                                evac_free[ev] = tok(qg)
                            elif role in ("hs", "b", "c"):
                                cfb = ({"hs": 0, "b": 32, "c": 36}[role]
                                       + off // 128)
                                dst = {"hs": hs_s, "b": bx_s, "c": cx_s}[role]
                                S(lambda s, bnk=bnk: s.activation(
                                    convin[:, 3:3 + TTW], bank(bnk),
                                    AF.Identity),
                                  waits=[MM_OK, conv_mm_last[0],
                                         halo_upd_last[0]])
                                inproj_bank_free[bnk] = tok(sch)
                                EVC = tok(sch)
                                V(lambda v, cfb=cfb: v.tensor_copy(
                                    convin[:, 0:3],
                                    halo[:, 3 * cfb:3 * cfb + 3]), waits=[EVC])
                                if tt == 0:
                                    for kk in range(4):
                                        V(lambda v, cfb=cfb, kk=kk:
                                          v.tensor_scalar(
                                              diag_t[kk][:, :], i128b[:, :],
                                              cw_t[:, 4 * cfb + kk:
                                                   4 * cfb + kk + 1],
                                              None, ALU.mult),
                                          waits=[conv_mm_last[0]])
                                CIN = tok(vch)
                                cgidx = conv_group
                                conv_group += 1
                                cbnk = 4 + cgidx % 2

                                def conv_emit(t, cbnk=cbnk):
                                    for kk in range(4):
                                        ins = t.matmul(
                                            bank(cbnk), diag_t[kk][:, :],
                                            convin[:, kk:kk + TTW],
                                            start=(kk == 0), stop=(kk == 3))
                                    return ins

                                T(conv_emit,
                                  waits=[CIN, conv_bank_free[cgidx % 2]])
                                conv_mm_last[0] = tok(tch)
                                CONV_OK = tok(tch)
                                V(lambda v, cfb=cfb: v.tensor_copy(
                                    halo[:, 3 * cfb:3 * cfb + 3],
                                    convin[:, TTW:TTW + 3]), waits=[CONV_OK])
                                halo_upd_last[0] = tok(vch)
                                co = cgidx % 2
                                S(lambda s, cbnk=cbnk, cfb=cfb, co=co:
                                  s.activation(convo_t[co][:, :], bank(cbnk),
                                               AF.Silu,
                                               bias=cb_t[:, cfb:cfb + 1]),
                                  waits=[CONV_OK, convo_free[co]])
                                conv_bank_free[cgidx % 2] = tok(sch)
                                GD(lambda g, dst=dst, off=off, tt=tt, t0=t0,
                                   co=co: g.dma_start(
                                       dst[off:off + 128,
                                           t0 + TTW * tt:t0 + TTW * (tt + 1)],
                                       convo_t[co][:, :]), waits=[tok(sch)])
                                convo_free[co] = tok(qg)
                            elif role == "dtwr":
                                S(lambda s, bnk=bnk: s.activation(
                                    dtw[0][:, :], bank(bnk)[0:64, :], AF.Abs,
                                    bias=dtbias_t[:, :]), waits=[MM_OK])
                                S(lambda s: s.activation(dtw[1][:, :],
                                                         dtw[0][:, :],
                                                         AF.Exp, scale=-1.0))
                                S(lambda s: s.activation(dtw[0][:, :],
                                                         dtw[1][:, :],
                                                         AF.Ln,
                                                         bias=onecol[0:64, :]))
                                S(lambda s, bnk=bnk: s.activation(
                                    dtw[1][:, :], bank(bnk)[0:64, :], AF.Relu,
                                    bias=dtbias_t[:, :]))
                                S(lambda s, bnk=bnk: s.activation(
                                    scf[:, :], bank(bnk)[64:128, :],
                                    AF.Identity))
                                inproj_bank_free[bnk] = tok(sch)
                                V(lambda v: v.tensor_add(dt32t[:, :],
                                                         dtw[0][:, :],
                                                         dtw[1][:, :]),
                                  waits=[tok(sch)])
                                V(lambda v: v.tensor_copy(dtbft[:, :],
                                                          dt32t[:, :]))
                                DTV = tok(vch)
                                GD(lambda g, tt=tt, t0=t0: g.dma_start(
                                    dt32_s[:, t0 + TTW * tt:
                                           t0 + TTW * (tt + 1)], dt32t[:, :]),
                                   waits=[DTV])
                                GD(lambda g, tt=tt, t0=t0: g.dma_start(
                                    dtb_s[:, t0 + TTW * tt:
                                          t0 + TTW * (tt + 1)], dtbft[:, :]),
                                   waits=[DTV])

                                for q in range(4):
                                    T(lambda t, q=q: t.transpose(
                                        bank(6)[:, 0:64],
                                        scf[:, 128 * q:128 * (q + 1)],
                                        i128f[0:64, 0:64]),
                                      waits=[tok(sch), tok(vch)])
                                    S(lambda s: s.activation(
                                        sct[:, :], bank(6)[:, 0:64],
                                        AF.Identity), waits=[tok(tch)])
                                    V(lambda v: v.max(out=mx1[:, :],
                                                      in_=sct[:, :]),
                                      waits=[tok(sch)])
                                    V(lambda v: v.match_replace(
                                        out=wk1[:, :], in_to_replace=mx1[:, :],
                                        in_values=sct[:, :], imm_value=-1e30))
                                    V(lambda v: v.max(out=mx2[:, :],
                                                      in_=wk1[:, :]))
                                    V(lambda v: v.tensor_single_scalar(
                                        msk[:, :], sct[:, :], mx2[:, 7:8],
                                        ALU.is_ge))
                                    V(lambda v: v.tensor_scalar(
                                        gsc[:, 0:1], mx1[:, 0:1], -1.0, None,
                                        ALU.mult))
                                    S(lambda s: s.activation(
                                        gex[:, :], sct[:, :], AF.Exp,
                                        bias=gsc[:, 0:1]), waits=[tok(vch)])
                                    V(lambda v: v.tensor_mul(
                                        gex[:, :], gex[:, :], msk[:, :]),
                                      waits=[tok(sch)])
                                    V(lambda v: v.reduce_sum(
                                        gsc[:, 1:2], gex[:, :], axis=AX.X))
                                    V(lambda v: v.reciprocal(gsc[:, 1:2],
                                                             gsc[:, 1:2]))
                                    V(lambda v: v.tensor_scalar(
                                        gex[:, :], gex[:, :], gsc[:, 1:2],
                                        None, ALU.mult))
                                    T(lambda t: t.transpose(
                                        bank(6)[0:64, 64:192], gex[:, :],
                                        i128f[:, :]),
                                      waits=[tok(vch), tok(sch)])
                                    S(lambda s, q=q, tt=tt: s.activation(
                                        g_fm[0:64, TTW * tt + 128 * q:
                                             TTW * tt + 128 * (q + 1)],
                                        bank(6)[0:64, 64:192], AF.Identity),
                                      waits=[tok(tch)])
                            elif role == "a":
                                bkr = off // 128
                                S(lambda s, bnk=bnk: s.activation(
                                    h_t[:, :], bank(bnk), AF.Identity),
                                  waits=[MM_OK, hg_spill_last[0]])
                                inproj_bank_free[bnk] = tok(sch)
                                H_OK = tok(sch)
                                T(lambda t, bkr=bkr, tt=tt: t.matmul(
                                    bank(6),
                                    selt[:, 128 * bkr:128 * (bkr + 1)],
                                    g_fm[0:64, TTW * tt:TTW * (tt + 1)],
                                    start=True, stop=True),
                                  waits=[H_OK, tok(vch)])
                                V(lambda v: v.tensor_mul(hg_t[:, :], h_t[:, :],
                                                         bank(6)),
                                  waits=[tok(tch)])
                                GD(lambda g, bkr=bkr, tt=tt, sup=sup:
                                   g.dma_start(
                                       hgq[1024 * sup + 128 * bkr:
                                           1024 * sup + 128 * (bkr + 1),
                                           TTW * tt:TTW * (tt + 1)],
                                       hg_t[:, :]), waits=[tok(vch)])
                                hg_spill_last[0] = tok(qg)
                    slab_consumed[k] = tch.n
                    if k + 2 < len(slab_order):
                        load_slab(k + 2)
                last_super_mm = tch.n
            P1_SPILLS = tok(qg)
            P1_V = tok(vch)
            P1_S = tok(sch)
            P1_T = tok(tch)

        with ExitStack() as es2:
            sb2 = SB(es2)
            hs_tok = [es2.enter_context(nc.sbuf_tensor(f"hs_tok{i}",
                                                       [128, 4096], BF16))
                      for i in range(2)]
            sg_tok = [es2.enter_context(nc.sbuf_tensor(f"sg_tok{i}",
                                                       [128, 4096], BF16))
                      for i in range(2)]
            dttokb = [es2.enter_context(nc.sbuf_tensor(f"dttokb{i}", [128, 64],
                                                       BF16)) for i in range(2)]
            dtfm = [sb2(f"dtfm{i}", [64, 128]) for i in range(2)]
            bfm = [es2.enter_context(nc.sbuf_tensor(f"bfm{i}", [128, 1024],
                                                    BF16)) for i in range(2)]
            cfm = [es2.enter_context(nc.sbuf_tensor(f"cfm{i}", [128, 1024],
                                                    BF16)) for i in range(2)]
            btok = [es2.enter_context(nc.sbuf_tensor(f"btok{i}", [128, 1024],
                                                     BF16)) for i in range(2)]
            arena = [sb2(f"arena{i}", [128, 1024]) for i in range(2)]
            lhsT_a = [sb2(f"lhsTa{i}", [128, 128]) for i in range(2)]
            sstate = sb2("sstate", [128, NH * HP])
            sstate_bf = es2.enter_context(nc.sbuf_tensor("sstate_bf",
                                                         [128, NH * HP], BF16))
            acs_fm = sb2("acs_fm", [64, 128])
            adt_fm = sb2("adt_fm", [64, 128])
            acs_tok = sb2("acs_tok", [128, 64])
            lndt = sb2("lndt", [128, 64])
            c4t = sb2("c4t", [128, 64])
            a_ct = sb2("a_ct", [1, 64])
            d2t = sb2("d2t", [128, 64])
            decayt = es2.enter_context(nc.sbuf_tensor("decayt", [128, 64], BF16))
            sdot = sb2("sdot", [128, 64])
            expat = sb2("expat", [128, 64])
            dtdec = es2.enter_context(nc.sbuf_tensor("dtdec", [128, 64], BF16))
            drest = es2.enter_context(nc.sbuf_tensor("drest", [128, 4096], BF16))
            xhatt = es2.enter_context(nc.sbuf_tensor("xhatt", [128, 4096], BF16))
            gm_m = es2.enter_context(nc.sbuf_tensor("gm_m", [128, 128], BF16))
            esb = [es2.enter_context(nc.sbuf_tensor(f"esb{i}", [128, 512], BF16))
                   for i in range(2)]
            eexp = [es2.enter_context(nc.sbuf_tensor(f"eexp{i}", [128, 512],
                                                     BF16)) for i in range(2)]
            mslab = [es2.enter_context(nc.sbuf_tensor(f"mslab{i}", [128, 512],
                                                      BF16)) for i in range(2)]
            ytmp = sb2("ytmp", [128, 512])
            ybf = es2.enter_context(nc.sbuf_tensor("ybf", [128, 4096], BF16))
            xgt = es2.enter_context(nc.sbuf_tensor("xgt", [128, 4096], BF16))
            nx1 = es2.enter_context(nc.sbuf_tensor("nx1", [128, 4096], BF16))
            nrmt = es2.enter_context(nc.sbuf_tensor("nrmt", [128, 4096], BF16))
            ssqt = sb2("ssqt", [128, 8])
            sqtrash = es2.enter_context(nc.sbuf_tensor("sqtrash", [128, 1024],
                                                       BF16))

            upper = sb2("upper", [128, 128])
            d_rows = es2.enter_context(nc.sbuf_tensor("d_rows", [128, NH],
                                                      BF16))
            nw_rows = es2.enter_context(nc.sbuf_tensor("nw_rows",
                                                       [128, NH * HP], BF16))
            GD(lambda g: g.dma_start(upper[:, :], c_upper))
            T(lambda t: t.matmul(PS[:, 0:NH], onesrow[0:1, 0:128],
                                 dvec_row[0:1, :], start=True, stop=True),
              waits=[SMALL, CONSTS_V, P1_T])
            V(lambda v: v.tensor_copy(d_rows[:, :], PS[:, 0:NH]),
              waits=[tok(tch), P1_V])
            for j in range(8):
                T(lambda t, j=j: t.matmul(bank(1), onesrow[0:1, 0:128],
                                          nwrow[0:1, 512 * j:512 * (j + 1)],
                                          start=True, stop=True),
                  waits=[tok(vch)])
                V(lambda v, j=j: v.tensor_copy(
                    nw_rows[:, 512 * j:512 * (j + 1)], bank(1)),
                  waits=[tok(tch)])
            V(lambda v: v.memset(sstate[:, :], 0.0))
            V(lambda v: v.memset(sstate_bf[:, :], 0.0))
            V(lambda v: v.memset(arena[0][:, :], 0.0))
            V(lambda v: v.memset(arena[1][:, :], 0.0))
            V(lambda v: v.memset(lhsT_a[0][:, :], 0.0))
            V(lambda v: v.memset(lhsT_a[1][:, :], 0.0))
            AR_Z = tok(vch)
            GD(lambda g: g.dma_start(arena[0][0:4, :], c_delta[:, 0:1024]),
               waits=[AR_Z])
            GD(lambda g: g.dma_start(arena[1][0:4, :], c_delta[:, 0:1024]))
            GD(lambda g: g.dma_start(lhsT_a[0][4:5, 0:128],
                                     onesrow[0:1, 0:128]))
            GD(lambda g: g.dma_start(lhsT_a[1][4:5, 0:128],
                                     onesrow[0:1, 0:128]))
            ARENA_INIT = tok(qg)

            GD(lambda g: g.dma_start(
                bxcx_sh[DynSlice(par(g, "g") * 512, 512), :], bx_s),
               waits=[P1_SPILLS])
            GD(lambda g: g.dma_start(
                bxcx_sh[DynSlice(1024 + par(g, "g") * 512, 512), :], cx_s))
            BXCX_W = tok(qg)
            GD(lambda g: g.dma_start(
                flag_sh[DynSlice(par(g, "g") + 2, 1), :], magic_t[:, :]),
               waits=[BXCX_W, FLAG_CLEARED])
            emit_spin(2, qsp2, "sp2")

            arena_free = [None, None]
            ebank_free = [None, None]
            mslab_free = [None, None]
            mslab_ok = [None, None]
            ydbank_free = [None, None]
            eexp_s = [None, None]
            yobank_free = [None, None]
            stbank_free = [None, None]
            sbf_ok = [None] * 8
            chunk_end_v = {}
            chunk_end_t = {}

            for c in range(NCH):
                p2 = c % 2
                ct0 = c * 128
                cw = [BXCX_W] if c == 0 else []
                if c >= 2:
                    cw = [(vch, chunk_end_v[c - 2]), (tch, chunk_end_t[c - 2])]
                Y(lambda s: s.dma_start_transpose(hs_tok[p2][:, :],
                                                  hs_s[:, ct0:ct0 + 128]),
                  waits=cw)
                Y(lambda s: s.dma_start_transpose(sg_tok[p2][:, :],
                                                  sg_s[:, ct0:ct0 + 128]))
                Y(lambda s: s.dma_start_transpose(dttokb[p2][:, :],
                                                  dtb_s[:, ct0:ct0 + 128]))
                Y(lambda s: s.dma_start_transpose(btok[p2][:, :],
                                                  bxcx_sh[0:1024,
                                                          ct0:ct0 + 128]))
                Y(lambda s: s.dma_start(dtfm[p2][:, :],
                                        dt32_s[:, ct0:ct0 + 128]))
                Y(lambda s: s.dma_start(
                    bfm[p2][:, :].rearrange("p (g t) -> p g t", g=8),
                    bxcx_sh[0:1024, ct0:ct0 + 128].rearrange(
                        "(g p) t -> p g t", p=128)))
                Y(lambda s: s.dma_start(
                    cfm[p2][:, :].rearrange("p (g t) -> p g t", g=8),
                    bxcx_sh[1024:2048, ct0:ct0 + 128].rearrange(
                        "(g p) t -> p g t", p=128)))
                LD = tok(qs)

                V(lambda v: v.tensor_scalar(adt_fm[:, :], dtfm[p2][:, :],
                                            acol[:, :], None, ALU.mult),
                  waits=[LD])
                V(lambda v: v.tensor_tensor_scan(acs_fm[:, :], adt_fm[:, :],
                                                 zero64[:, :], 0.0,
                                                 ALU.add, ALU.add))
                ACSFM_V = tok(vch)
                GD(lambda g: g.dma_start(acsflat, acs_fm[:, :]),
                   waits=[ACSFM_V, arena_free[0], arena_free[1]])
                ACSFLAT_W = tok(qg)
                T(lambda t: t.transpose(bank(1)[:, 0:64], acs_fm[:, :],
                                        i128f[0:64, 0:64]),
                  waits=[ACSFM_V, ydbank_free[1]])
                S(lambda s: s.activation(acs_tok[:, :], bank(1)[:, 0:64],
                                         AF.Identity), waits=[tok(tch)])
                ACS_S = tok(sch)
                S(lambda s: s.activation(lndt[:, :], dttokb[p2][:, :], AF.Ln),
                  waits=[LD])
                V(lambda v: v.tensor_sub(c4t[:, :], acs_tok[:, :],
                                         lndt[:, :]), waits=[tok(sch)])
                C4_V = tok(vch)
                T(lambda t: t.matmul(bank(1)[0:1, 64:128], e127[:, :],
                                     acs_tok[:, :], start=True, stop=True),
                  waits=[ACS_S])
                S(lambda s: s.activation(a_ct[:, :], bank(1)[0:1, 64:128],
                                         AF.Identity), waits=[tok(tch)])
                T(lambda t: t.matmul(bank(1)[:, 128:192], onesrow[0:1, 0:128],
                                     a_ct[0:1, :], start=True, stop=True),
                  waits=[tok(sch)])
                V(lambda v: v.tensor_sub(d2t[:, :], bank(1)[:, 128:192],
                                         acs_tok[:, :]), waits=[tok(tch)])
                D2_V = tok(vch)
                S(lambda s: s.activation(decayt[:, :], d2t[:, :], AF.Exp),
                  waits=[D2_V])
                S(lambda s: s.activation(sdot[:, :], acs_tok[:, :], AF.Exp))
                S(lambda s: s.activation(expat[:, :], bank(1)[:, 128:192],
                                         AF.Exp))
                EXPA_S = tok(sch)
                V(lambda v: v.tensor_mul(dtdec[:, :], dttokb[p2][:, :],
                                         decayt[:, :]), waits=[tok(sch)])
                V(lambda v: v.tensor_tensor(
                    drest[:, :].rearrange("t (h p) -> t h p", p=HP),
                    hs_tok[p2][:, :].rearrange("t (h p) -> t h p", p=HP),
                    d_rows[:, :].unsqueeze(2).broadcast_to([128, NH, HP]),
                    ALU.mult))
                V(lambda v: v.tensor_tensor(
                    xhatt[:, :].rearrange("t (h p) -> t h p", p=HP),
                    hs_tok[p2][:, :].rearrange("t (h p) -> t h p", p=HP),
                    dtdec[:, :].unsqueeze(2).broadcast_to([128, NH, HP]),
                    ALU.mult))
                XHAT_V = tok(vch)

                for sg in range(8):
                    sgp = sg % 2
                    ydb = bank(0 + sgp)
                    yob = bank(2 + sgp)
                    stb = bank(4 + sgp)

                    def gview(ap4096):
                        return ap4096.rearrange(
                            "t (k g2 p) -> t g2 k p", g2=8, p=HP)[:, sg, :, :]

                    T(lambda t, sg=sg: t.matmul(
                        yob[:, 0:128],
                        bfm[p2][:, 128 * sg:128 * (sg + 1)],
                        cfm[p2][:, 128 * sg:128 * (sg + 1)],
                        start=True, stop=True),
                      waits=[LD, yobank_free[sgp]])
                    V(lambda v, sg=sg: v.tensor_tensor(
                        gm_m[:, :], yob[:, 0:128],
                        upper[:, :], ALU.mult), waits=[tok(tch), tok(pch)])
                    GM_V = tok(vch)

                    GD(lambda g, sg=sg: g.dma_start(
                        arena[sgp][4:5, :].rearrange("o (k t) -> o k t", k=8),
                        acsflat.rearrange("(k g2) t -> g2 k t", g2=8)[
                            sg:sg + 1, :, :].squeeze(0).unsqueeze(0)),
                       waits=[ACSFLAT_W, ARENA_INIT, arena_free[sgp]])
                    AR_OK = tok(qg)

                    for si in range(2):
                        j = 2 * sg + si
                        T(lambda t, sg=sg, si=si: t.transpose(
                            stb[0:4, 384:512],
                            c4t[:, :].rearrange("t (k g2) -> t g2 k", g2=8)[
                                :, sg, 4 * si:4 * si + 4],
                            i128f[:, :]),
                          waits=[C4_V, EXPA_S, D2_V, tok(vch),
                                 stbank_free[sgp]])
                        V(lambda v: v.tensor_copy(
                            lhsT_a[j % 2][0:4, :], stb[0:4, 384:512]),
                          waits=[tok(tch)])
                        LH_V = tok(vch)
                        T(lambda t, j=j, sgp=sgp, si=si: t.matmul(
                            bank(6 + j % 2), lhsT_a[j % 2][:, :],
                            arena[sgp][:, 512 * si:512 * (si + 1)],
                            start=True, stop=True),
                          waits=[LH_V, AR_OK, ebank_free[j % 2]])
                        EMM = tok(tch)
                        if si == 1:
                            arena_free[sgp] = EMM
                        V(lambda v, j=j: v.tensor_scalar(
                            esb[j % 2][:, :], bank(6 + j % 2), 30.0, None,
                            ALU.min), waits=[EMM, eexp_s[j % 2]])
                        ebank_free[j % 2] = tok(vch)
                        S(lambda s, j=j: s.activation(
                            eexp[j % 2][:, :], esb[j % 2][:, :], AF.Exp),
                          waits=[tok(vch), tok(pch)])
                        eexp_s[j % 2] = tok(sch)
                        P(lambda p, j=j, si=si: p.tensor_tensor(
                            mslab[si][:, :].rearrange("s (a l) -> s a l", a=4),
                            eexp[j % 2][:, :].rearrange("s (a l) -> s a l",
                                                        a=4),
                            gm_m[:, :].unsqueeze(1)
                            .broadcast_to([128, 4, 128]), ALU.mult),
                          waits=[tok(sch), GM_V, mslab_free[si]])
                        mslab_ok[si] = tok(pch)

                    T(lambda t, sg=sg: t.matmul(
                        ydb, i128b[:, :], gview(drest[:, :]),
                        start=True, stop=True),
                      waits=[XHAT_V, ydbank_free[sgp], EXPA_S, ACS_S])

                    def ydiag_emit(t, sg=sg):
                        for k in range(8):
                            h = sg + 8 * k
                            ins = t.matmul(
                                ydb[:, 64 * k:64 * (k + 1)],
                                mslab[k // 4][:, 128 * (k % 4):
                                              128 * (k % 4 + 1)],
                                hs_tok[p2][:, 64 * h:64 * (h + 1)],
                                start=False, stop=True)
                        return ins
                    T(ydiag_emit, waits=[mslab_ok[0], mslab_ok[1]])
                    YD = tok(tch)
                    mslab_free[0] = YD
                    mslab_free[1] = YD

                    T(lambda t, sg=sg: t.matmul(
                        yob, cfm[p2][:, 128 * sg:128 * (sg + 1)],
                        gview(sstate_bf[:, :]),
                        start=True, stop=True),
                      waits=[LD, yobank_free[sgp], sbf_ok[sg], GM_V])
                    YO = tok(tch)

                    T(lambda t, sg=sg: t.matmul(
                        stb, btok[p2][:, 128 * sg:128 * (sg + 1)],
                        gview(xhatt[:, :]),
                        start=True, stop=True),
                      waits=[XHAT_V, LD, tok(vch), stbank_free[sgp]])
                    ST = tok(tch)

                    V(lambda v, sg=sg: v.tensor_tensor(
                        ytmp[:, :].rearrange("t (a p) -> t a p", p=HP),
                        yob.rearrange("t (a p) -> t a p", p=HP),
                        sdot[:, :].rearrange("t (k g2) -> t g2 k", g2=8)[
                            :, sg, :].unsqueeze(2)
                        .broadcast_to([128, 8, HP]), ALU.mult), waits=[YO])
                    yobank_free[sgp] = tok(vch)
                    V(lambda v, sg=sg: v.tensor_tensor(
                        gview(ybf[:, :]),
                        ytmp[:, :].rearrange("t (a p) -> t a p", p=HP),
                        ydb.rearrange("t (a p) -> t a p", p=HP),
                        ALU.add), waits=[YD])
                    ydbank_free[sgp] = tok(vch)
                    V(lambda v, sg=sg: v.tensor_tensor(
                        gview(sstate[:, :]), gview(sstate[:, :]),
                        expat[:, :].rearrange("t (k g2) -> t g2 k", g2=8)[
                            :, sg, :].unsqueeze(2)
                        .broadcast_to([128, 8, HP]), ALU.mult),
                      waits=[EXPA_S])
                    V(lambda v, sg=sg: v.tensor_tensor(
                        gview(sstate[:, :]), gview(sstate[:, :]),
                        stb.rearrange("n (a p) -> n a p", p=HP),
                        ALU.add), waits=[ST])
                    stbank_free[sgp] = tok(vch)
                    V(lambda v, sg=sg: v.tensor_copy(
                        gview(sstate_bf[:, :]), gview(sstate[:, :])))
                    sbf_ok[sg] = tok(vch)

                V(lambda v: v.tensor_mul(xgt[:, :], ybf[:, :],
                                         sg_tok[p2][:, :]))
                XG_V = tok(vch)
                for g4 in range(4):
                    S(lambda s, g4=g4: s.activation(
                        sqtrash[:, :], xgt[:, 1024 * g4:1024 * (g4 + 1)],
                        AF.Square, accum_out=ssqt[:, g4:g4 + 1]),
                      waits=[XG_V])
                S(lambda s: s.activation(ssqt[:, 4:8], ssqt[:, 0:4], AF.Ln,
                                         scale=1.0 / 1024.0,
                                         bias=epscol[:, :]))
                S(lambda s: s.activation(ssqt[:, 0:4], ssqt[:, 4:8], AF.Exp,
                                         scale=-0.5))
                V(lambda v: v.tensor_tensor(
                    nx1[:, :].rearrange("t (a f) -> t a f", a=4),
                    xgt[:, :].rearrange("t (a f) -> t a f", a=4),
                    ssqt[:, 0:4].unsqueeze(2).broadcast_to([128, 4, 1024]),
                    ALU.mult), waits=[tok(sch)])
                V(lambda v: v.tensor_mul(nrmt[:, :], nx1[:, :], nw_rows[:, :]))
                NRM_V = tok(vch)
                GD(lambda g, c=c: g.dma_start(
                    normed_loc[128 * c:128 * (c + 1), :],
                    nrmt[:, :]), waits=[NRM_V])
                chunk_end_v[c] = vch.n
                chunk_end_t[c] = tch.n
            P2_SPILLS = tok(qg)
            P2_T = tok(tch)

        GD(lambda g: g.dma_start(
            normed_sh[:, DynSlice(par(g, "g") * 4096, 4096)],
            normed_loc), waits=[P2_SPILLS])
        NORMED_SH_W = tok(qg)
        GD(lambda g: g.dma_start(flag_sh[parity["g"], :].unsqueeze(0),
                                 magic_t[:, :]),
           waits=[NORMED_SH_W, FLAG_CLEARED])

        emit_spin(0, qsp, "sp1")
        Y2(lambda s: s.dma_start(
            normed_loc3,
            normed_sh[DynSlice(par(s, "s") * 2048, 2048), :]),
           waits=[NORMED_SH_W])
        Y2(lambda s: s.dma_start(
            hgq_loc, hgq[DynSlice(par(s, "s") * 1024, 1024), :]),
           waits=[P1_SPILLS])
        Y2(lambda s: s.dma_start(
            x_loc, x_in[DynSlice(par(s, "s") * 2048, 2048), :]))
        BULK_OK = tok(qs2)

        with ExitStack() as es3:
            nrm3 = [es3.enter_context(nc.sbuf_tensor(f"nrm3_{i}", [128, TTW],
                                                     BF16)) for i in range(64)]
            wp3 = [es3.enter_context(nc.sbuf_tensor(f"wp3_{i}", [128, 8192],
                                                    BF16)) for i in range(2)]
            at_t = [es3.enter_context(nc.sbuf_tensor(f"at_t{i}", [128, TTW],
                                                     BF16)) for i in range(2)]
            p3bank_free = [None] * 4
            at_free = [None, None]
            wp3_used = []
            p3_tile_done = []
            for tt3 in range(4):
                NRM0 = qs2.n
                for ic in range(64):
                    w = []
                    if tt3 == 0 and ic == 0:
                        w = [BULK_OK]
                    if tt3 > 0 and ic == 0:
                        w = [(tch, p3_tile_done[tt3 - 1])]
                    Y2(lambda s, tt3=tt3, ic=ic: s.dma_start_transpose(
                        nrm3[ic][:, :],
                        normed_loc3[512 * tt3:512 * (tt3 + 1),
                                    128 * ic:128 * (ic + 1)]), waits=w)
                NRM_OK = (qs2, NRM0 + 16 * 64)
                for og in range(32):
                    w = [WOT_DONE] if (tt3 == 0 and og == 0) else []
                    gi = tt3 * 32 + og
                    if gi >= 2:
                        w.append((tch, wp3_used[gi - 2]))
                    GD(lambda g, og=og: g.dma_start(
                        wp3[og % 2][:, :].rearrange("p (ic f) -> p ic f",
                                                    f=128),
                        woT[og, :, :, :].transpose([1, 0, 2])),
                       waits=w)
                    WP3 = tok(qg)
                    bnk = og % 4

                    def p3_emit(t, og=og, bnk=bnk):
                        for ic in range(64):
                            ins = t.matmul(
                                bank(bnk),
                                wp3[og % 2][:, 128 * ic:128 * (ic + 1)],
                                nrm3[ic][:, :],
                                start=(ic == 0), stop=(ic == 63))
                        return ins
                    T(p3_emit, waits=[WP3, NRM_OK, p3bank_free[bnk]])
                    wp3_used.append(tch.n)
                    PMM = tok(tch)
                    S(lambda s, og=og, bnk=bnk: s.activation(
                        at_t[og % 2][:, :], bank(bnk), AF.Identity),
                      waits=[PMM, at_free[og % 2]])
                    p3bank_free[bnk] = tok(sch)
                    GD(lambda g, og=og, tt3=tt3: g.dma_start(
                        attn_s[128 * og:128 * (og + 1),
                               TTW * tt3:TTW * (tt3 + 1)],
                        at_t[og % 2][:, :]), waits=[tok(sch)])
                    at_free[og % 2] = tok(qg)
                p3_tile_done.append(tch.n)
            P3_SPILLS = tok(qg)
            P3_T = tok(tch)

        with ExitStack() as es3b:
            bprime = [es3b.enter_context(nc.sbuf_tensor(f"bprime{i}",
                                                        [128, 4096], BF16))
                      for i in range(8)]
            hgt3 = [es3b.enter_context(nc.sbuf_tensor(f"hgt3_{i}", [128, 2048],
                                                      BF16)) for i in range(8)]
            ff_t = [es3b.enter_context(nc.sbuf_tensor(f"ff_t{i}", [128, 1024],
                                                      BF16)) for i in range(2)]
            BP0 = qs2.n
            for kb in range(8):
                w = [CASTS, P3_T] if kb == 0 else []
                Y2(lambda s, kb=kb: s.dma_start(
                    bprime[kb][:, :], bnat[128 * kb:128 * (kb + 1), :]),
                   waits=w)
            for bkr in range(8):
                Y2(lambda s, bkr=bkr: s.dma_start(
                    hgt3[bkr][:, :],
                    hgq_loc[128 * bkr:128 * (bkr + 1), :]),
                   waits=[BULK_OK])
            FFL = (qs2, BP0 + 16 * 16)
            ffbank_free = [None, None]
            ff_free = [None, None]
            for ds in range(4):
                for tc in range(16):
                    idx = ds * 16 + tc
                    pb = 4 + 2 * (idx % 2)

                    def ffb_emit(t, ds=ds, tc=tc, pb=pb):
                        for kr in range(8):
                            for hf in range(2):
                                ins = t.matmul(
                                    bank(pb + hf),
                                    hgt3[kr][:, 128 * tc:128 * (tc + 1)],
                                    bprime[kr][:, 1024 * ds + 512 * hf:
                                               1024 * ds + 512 * (hf + 1)],
                                    start=(kr == 0), stop=(kr == 7))
                        return ins
                    T(ffb_emit, waits=[FFL, ffbank_free[idx % 2]])
                    FMM = tok(tch)
                    S(lambda s, pb=pb, idx=idx: s.activation(
                        ff_t[idx % 2][:, :], bank(pb, 2), AF.Identity,
                        scale=2.0), waits=[FMM, ff_free[idx % 2]])
                    ffbank_free[idx % 2] = tok(sch)
                    GD(lambda g, ds=ds, tc=tc, idx=idx: g.dma_start(
                        ffn_s[128 * tc:128 * (tc + 1),
                              1024 * ds:1024 * (ds + 1)],
                        ff_t[idx % 2][:, :]), waits=[tok(sch)])
                    ff_free[idx % 2] = tok(qg)
            FFN_SPILLS = tok(qg)
            FFN_T = tok(tch)

        with ExitStack() as es4:
            sb4 = SB(es4)
            x4 = [sb4(f"x4_{i}", [128, D]) for i in range(2)]
            at4 = [es4.enter_context(nc.sbuf_tensor(f"at4_{i}", [128, D], BF16))
                   for i in range(2)]
            ff4 = [es4.enter_context(nc.sbuf_tensor(f"ff4_{i}", [128, D], BF16))
                   for i in range(2)]
            o4 = [sb4(f"o4_{i}", [128, D]) for i in range(2)]

            p4_done = []
            for tc in range(16):
                pq = tc % 2
                w = [P3_SPILLS, FFN_SPILLS, FFN_T] if tc == 0 else []
                if tc >= 2:
                    w = [(vch, p4_done[tc - 2])]
                Y2(lambda s, tc=tc: s.dma_start(
                    x4[pq][:, :], x_loc[128 * tc:128 * (tc + 1), :]),
                   waits=w)
                Y2(lambda s, tc=tc: s.dma_start_transpose(
                    at4[pq][:, :], attn_s[:, 128 * tc:128 * (tc + 1)]))
                Y2(lambda s, tc=tc: s.dma_start(
                    ff4[pq][:, :], ffn_s[128 * tc:128 * (tc + 1), :]))
                L4 = tok(qs2)
                V(lambda v: v.tensor_add(o4[pq][:, :], x4[pq][:, :],
                                         at4[pq][:, :]), waits=[L4])
                V(lambda v: v.tensor_add(o4[pq][:, :], o4[pq][:, :],
                                         ff4[pq][:, :]))
                p4_done.append(vch.n)
                GD(lambda g, tc=tc: g.dma_start(
                    out[128 * tc:128 * (tc + 1), :], o4[pq][:, :]),
                   waits=[tok(vch)])
            OUT_DONE = tok(qg)

        for dap, src in dumps:
            GD(lambda g, dap=dap, src=src: g.dma_start(dap, src),
               waits=[OUT_DONE])
        DUMP_DONE = tok(qg)

        @block.sync
        def _(s):
            s.wait_ge(qg.h, DUMP_DONE[1])
            s.wait_ge(qs2.h, qs2.n)
    return nc


def make_consts():
    i128 = np.eye(128, dtype=np.float32)
    upper01 = (np.arange(128)[None, :] >= np.arange(128)[:, None]).astype(
        np.float32)
    delta = np.zeros((4, 2048), np.float32)
    for sl in range(4):
        for k in range(4):
            delta[k, 512 * sl + 128 * k:512 * sl + 128 * (k + 1)] = -1.0
    e127 = np.zeros((128, 1), np.float32)
    e127[127, 0] = 1.0
    sel = np.zeros((64, 1024), np.float32)
    for b in range(8):
        for j in range(128):
            sel[8 * b + j // 16, 128 * b + j] = 1.0
    return {
        "c_i128f": i128, "c_i128b": i128.astype(bfdt), "c_upper": upper01,
        "c_delta": delta, "c_e127": e127, "c_sel": sel.astype(bfdt),
    }


def shard_inputs(ins):
    W = np.ascontiguousarray(ins["L0_attn_in_proj_w"], np.float32)
    A = np.ascontiguousarray(ins["L0_ffn_A"], np.float32).reshape(1024, 4096)
    Wr = np.ascontiguousarray(ins["L0_ffn_Wr"], np.float32)
    cw = np.ascontiguousarray(ins["L0_attn_conv_w"][:, 0, :], np.float32)
    cb = np.ascontiguousarray(ins["L0_attn_conv_b"], np.float32)
    wout = np.ascontiguousarray(ins["L0_attn_out_proj_w"], np.float32)
    wotile = np.ascontiguousarray(
        wout.reshape(32, 128, 64, 128).transpose(0, 2, 3, 1))
    bp = np.ascontiguousarray(
        ins["L0_ffn_B"].transpose(0, 2, 1).reshape(1024, 4096), np.float32)
    consts = make_consts()
    maps = []
    for c in range(8):
        b, hh = divmod(c, 2)
        g0 = 4096 * hh
        wcat = np.concatenate([
            W[g0:g0 + 4096],
            W[8192 + g0:8192 + g0 + 4096],
            W[16384 + 512 * hh:16384 + 512 * hh + 512],
            W[17408 + 512 * hh:17408 + 512 * hh + 512],
            A,
            W[18432 + 64 * hh:18432 + 64 * hh + 64],
            Wr,
            np.zeros((128, 4096), np.float32),
        ], axis=0)
        convw_c = np.concatenate([
            cw[g0:g0 + 4096], cw[8192 + 512 * hh:8192 + 512 * hh + 512],
            cw[9216 + 512 * hh:9216 + 512 * hh + 512]], axis=0)
        convb_c = np.concatenate([
            cb[g0:g0 + 4096], cb[8192 + 512 * hh:8192 + 512 * hh + 512],
            cb[9216 + 512 * hh:9216 + 512 * hh + 512]], axis=0).reshape(-1, 1)
        wtile = np.ascontiguousarray(
            wcat.reshape(NSLAB, 128, 32, 128).transpose(0, 3, 2, 1)
            .reshape(NSLAB, 128, 4096))
        m = {
            "x": np.ascontiguousarray(ins["L0_attn_hidden_states"][b],
                                      np.float32),
            "wtile": wtile,
            "convw": np.ascontiguousarray(convw_c),
            "convb": np.ascontiguousarray(convb_c),
            "dtbias": np.ascontiguousarray(
                ins["L0_attn_dt_bias"][64 * hh:64 * hh + 64].reshape(64, 1),
                np.float32),
            "alog": np.ascontiguousarray(
                ins["L0_attn_A_log"][64 * hh:64 * hh + 64].reshape(64, 1),
                np.float32),
            "dvec": np.ascontiguousarray(
                ins["L0_attn_D"][64 * hh:64 * hh + 64].reshape(1, 64),
                np.float32),
            "nw": np.ascontiguousarray(
                ins["L0_attn_norm_weight"][4096 * hh:4096 * hh + 4096]
                .reshape(1, 4096), np.float32),
            "wotile": wotile,
            "bmat": bp,
        }
        m.update(consts)
        maps.append(m)
    return maps


def kernel(L0_attn_hidden_states, L0_attn_in_proj_w, L0_attn_conv_w,
           L0_attn_conv_b, L0_attn_dt_bias, L0_attn_A_log, L0_attn_D,
           L0_attn_norm_weight, L0_attn_out_proj_w, L0_ffn_A, L0_ffn_B,
           L0_ffn_Wr):
    ins = dict(L0_attn_hidden_states=np.asarray(L0_attn_hidden_states),
               L0_attn_in_proj_w=np.asarray(L0_attn_in_proj_w),
               L0_attn_conv_w=np.asarray(L0_attn_conv_w),
               L0_attn_conv_b=np.asarray(L0_attn_conv_b),
               L0_attn_dt_bias=np.asarray(L0_attn_dt_bias),
               L0_attn_A_log=np.asarray(L0_attn_A_log),
               L0_attn_D=np.asarray(L0_attn_D),
               L0_attn_norm_weight=np.asarray(L0_attn_norm_weight),
               L0_attn_out_proj_w=np.asarray(L0_attn_out_proj_w),
               L0_ffn_A=np.asarray(L0_ffn_A),
               L0_ffn_B=np.asarray(L0_ffn_B),
               L0_ffn_Wr=np.asarray(L0_ffn_Wr))
    maps = shard_inputs(ins)
    nc = build_nc()
    res = run_bass_kernel_spmd(nc, maps, core_ids=list(range(8)))
    outp = np.empty((4, SEQ, D), np.float32)
    for c in range(8):
        b, hh = divmod(c, 2)
        outp[b, 2048 * hh:2048 * (hh + 1), :] = res.results[c]["out"]
    return outp


# version 07
# speedup vs baseline: 1.5374x; 1.5374x over previous
import numpy as np
import ml_dtypes
import concourse.bass as bass
import concourse.mybir as mybir
from concourse.bass_utils import run_bass_kernel_spmd
from concourse.bass_types import DynSlice
from contextlib import ExitStack

F32 = mybir.dt.float32
BF16 = mybir.dt.bfloat16
AF = mybir.ActivationFunctionType
ALU = mybir.AluOpType
AX = mybir.AxisListType
bfdt = ml_dtypes.bfloat16

SEQ, D = 4096, 4096
NH, HP, NG, CH = 64, 64, 4, 128
NCH = SEQ // CH
TOK3 = 2048
EPS = 1e-5
R_GATE, R_HS, R_B, R_C, R_A, R_DTWR, R_PAD = 0, 4096, 8192, 8704, 9216, 10240, 10368
WROWS = 10496
FW = 128
NSLAB = WROWS // FW
CONVCH, NCONVFB = 5120, 40
MAGIC = 987654.0
TSUP, NSUP, TTW, NTT = 2048, 2, 512, 4


def build_nc(dump=()):
    nc = bass.Bass("TRN2", target_bir_lowering=False)

    def inp(name, shape, dt=F32):
        return nc.dram_tensor(name, shape, dt, kind="ExternalInput").ap()

    x_in = inp("x", [SEQ, D])
    wtile = inp("wtile", [NSLAB, 128, 4096])
    convw = inp("convw", [CONVCH, 4])
    convb = inp("convb", [CONVCH, 1])
    dtbias = inp("dtbias", [NH, 1])
    alog = inp("alog", [NH, 1])
    dvec = inp("dvec", [1, NH])
    nw = inp("nw", [1, NH * HP])
    wotile = inp("wotile", [32, 128, 8192])
    bmat = inp("bmat", [1024, 4096])
    c_i128f = inp("c_i128f", [128, 128])
    c_i128b = inp("c_i128b", [128, 128], BF16)
    c_upper = inp("c_upper", [128, 128])
    c_delta = inp("c_delta", [4, 2048])
    c_e127 = inp("c_e127", [128, 1])
    c_sel = inp("c_sel", [64, 1024], BF16)

    out = nc.dram_tensor("out", [TOK3, D], F32, kind="ExternalOutput").ap()

    def scr(name, shape, dt=BF16, **kw):
        return nc.dram_tensor(name, shape, dt, kind="Internal", **kw).ap()

    wT = scr("wT", [NSLAB, 128, 4096])
    woT = scr("woT", [32, 128, 8192])
    bnat = scr("bnat", [1024, 4096])
    xn_tok = scr("xn_tok", [SEQ, D])
    sg_s = scr("sg_s", [4096, SEQ])
    hs_s = scr("hs_s", [4096, SEQ])
    bx_s = scr("bx_s", [512, SEQ])
    cx_s = scr("cx_s", [512, SEQ])
    dt32_s = scr("dt32_s", [NH, SEQ], F32)
    dtb_s = scr("dtb_s", [NH, SEQ])
    hgq = scr("hgq", [2, 128, 8, TSUP])
    x_loc = scr("x_loc", [TOK3, D], F32)
    acsflat = scr("acsflat", [64, 128], F32)
    attn_s = scr("attn_s", [D, TOK3])
    ffn_s = scr("ffn_s", [TOK3, D])
    normed_loc = scr("normed_loc", [SEQ, 4096])
    normed_loc3 = scr("normed_loc3", [TOK3, 8192])
    normed_sh = scr("normed_sh", [SEQ, 8192], addr_space="Shared")
    flag_sh = scr("flag_sh", [4, 64], F32, addr_space="Shared")
    bxcx_sh = scr("bxcx_sh", [2048, SEQ], addr_space="Shared")

    dmap = {"xn_tok": xn_tok, "sg_s": sg_s, "hs_s": hs_s, "bx_s": bx_s,
            "cx_s": cx_s, "dt32_s": dt32_s, "dtb_s": dtb_s, "hgq": hgq,
            "attn_s": attn_s, "ffn_s": ffn_s, "normed_loc": normed_loc,
            "normed_loc3": normed_loc3, "x_loc": x_loc}
    dumps = [(nc.dram_tensor("dump_" + n, list(dmap[n].shape), dmap[n].dtype,
                             kind="ExternalOutput").ap(), dmap[n]) for n in dump]

    es = ExitStack()
    with es:
        def SB(stack):
            def alloc(name, shape, dt=F32):
                return stack.enter_context(nc.sbuf_tensor(name, shape, dt))
            return alloc

        sbp = SB(es)
        i128f = sbp("i128f", [128, 128])
        i128b = sbp("i128b", [128, 128], BF16)
        e127 = sbp("e127", [128, 1])
        selt = sbp("selt", [64, 1024], BF16)
        onesrow = sbp("onesrow", [1, 512])
        acol = sbp("acol", [NH, 1])
        dtbias_t = sbp("dtbias_t", [NH, 1])
        dvec_row = sbp("dvec_row", [1, NH])
        nwrow = sbp("nwrow", [1, NH * HP])
        cw_t = sbp("cw_t", [128, 4 * NCONVFB])
        cb_t = sbp("cb_t", [128, NCONVFB])
        halo = sbp("halo", [128, 3 * NCONVFB], BF16)
        magic_t = sbp("magic_t", [1, 64])
        zrow = sbp("zrow", [1, 64])
        zero64 = sbp("zero64", [NH, 128])
        flagrd = sbp("flagrd", [1, 64])
        epscol = sbp("epscol", [128, 1])
        onecol = sbp("onecol", [128, 1])

        PS = es.enter_context(nc.psum_tensor("PS", [128, 4096], F32))

        def bank(i, n=1):
            return PS[:, 512 * i:512 * (i + n)]

        class Sem:
            def __init__(self, name):
                self.h = es.enter_context(nc.semaphore(name))
                self.n = 0

        qg, qs, qa, qs2 = Sem("qg"), Sem("qs"), Sem("qa"), Sem("qs2")
        qsp = Sem("qsp")
        qsp2 = Sem("qsp2")
        vch, sch, pch, tch = Sem("vch"), Sem("sch"), Sem("pch"), Sem("tch")
        block = es.enter_context(nc.Block())

        def _mk(engsel, chain, selfsync):
            def op(emit, waits=(), inc=None, k=1):
                def body(e):
                    for wt in waits:
                        if wt is None:
                            continue
                    for wt in waits:
                        if wt is None:
                            continue
                        sem, n = wt
                        if n > 0:
                            e.wait_ge(sem.h, n)
                    ins = emit(e)
                    tgt = chain if inc is None else inc
                    if tgt is not None:
                        ins.then_inc(tgt.h, k)
                        tgt.n += k
                        if selfsync and tgt is chain:
                            e.wait_ge(chain.h, chain.n)
                engsel(body)
            return op

        V = _mk(block.vector, vch, True)
        S = _mk(block.scalar, sch, True)
        P = _mk(block.gpsimd, pch, True)
        G = _mk(block.gpsimd, qg, False)

        def mkdma(engsel, sem):
            def op(emit, waits=()):
                def body(e):
                    for wt in waits:
                        if wt is None:
                            continue
                    for wt in waits:
                        if wt is None:
                            continue
                        s2, n = wt
                        if n > 0:
                            e.wait_ge(s2.h, n)
                    emit(e).then_inc(sem.h, 16)
                    sem.n += 16
                engsel(body)
            return op

        GD = mkdma(block.gpsimd, qg)
        Y = mkdma(block.sync, qs)
        Y2 = mkdma(block.sync, qs2)
        SD = mkdma(block.scalar, qa)

        def T(emit, waits=()):
            def body(t):
                for wt in waits:
                    if wt is None:
                        continue
                    sem, n = wt
                    if n > 0:
                        t.wait_ge(sem.h, n)
                emit(t).then_inc(tch.h, 1)
                tch.n += 1
            block.tensor(body)

        def tok(sem):
            return (sem, sem.n)

        for dst, src in [(i128f, c_i128f), (i128b, c_i128b),
                         (e127, c_e127), (selt, c_sel), (acol, alog),
                         (dtbias_t, dtbias), (dvec_row, dvec), (nwrow, nw)]:
            GD(lambda g, dst=dst, src=src: g.dma_start(dst[:, :], src))
        for fb in range(NCONVFB):
            GD(lambda g, fb=fb: g.dma_start(cw_t[:, 4 * fb:4 * fb + 4],
                                            convw[128 * fb:128 * fb + 128, :]))
            GD(lambda g, fb=fb: g.dma_start(cb_t[:, fb:fb + 1],
                                            convb[128 * fb:128 * fb + 128, :]))
        SMALL = tok(qg)

        V(lambda v: v.memset(zrow[:, :], 0.0))
        V(lambda v: v.memset(magic_t[:, :], MAGIC))
        V(lambda v: v.memset(zero64[:, :], 0.0))
        V(lambda v: v.memset(halo[:, :], 0.0))
        V(lambda v: v.memset(onesrow[:, :], 1.0))
        V(lambda v: v.memset(epscol[:, :], EPS))
        V(lambda v: v.memset(onecol[:, :], 1.0))
        CONSTS_V = tok(vch)

        parity = {}

        def _par(engname, engsel, et):
            def body(e):
                r = nc.alloc_register(et, engname + "_par")
                e.load(r, nc.partition_id_tensor.ap())
                e.reg_alu(r, r, 1, ALU.bitwise_and)
                parity[engname] = e.compute_val(r)
            engsel(body)

        _par("g", block.gpsimd, mybir.EngineType.Pool)
        _par("s", block.sync, mybir.EngineType.SP)

        parity_snap = {}

        def par(e, name):
            if name not in parity_snap:
                parity_snap[name] = e.snap(parity[name], min_val=0, max_val=1)
            return parity_snap[name]

        Y(lambda s: s.dma_start(
            flag_sh[DynSlice(par(s, "s"), 1), :], zrow[:, :]),
          waits=[CONSTS_V])
        Y(lambda s: s.dma_start(
            flag_sh[DynSlice(par(s, "s") + 2, 1), :], zrow[:, :]))
        FLAG_CLEARED = tok(qs)

        MAGIC_U32 = int(np.float32(MAGIC).view(np.uint32))

        def emit_spin(rowbase, spinsem, tag):
            def body(s):
                with s.register(tag + "_flv") as flv, \
                     s.register(tag + "_oth") as oth, \
                     s.register(tag + "_cnt") as cnt:
                    s.load(oth, nc.partition_id_tensor.ap())
                    s.reg_alu(oth, oth, 1, ALU.bitwise_and)
                    s.reg_alu(oth, oth, 1, ALU.bitwise_xor)
                    s.reg_alu(oth, oth, rowbase, ALU.add)
                    o_v = s.snap(s.compute_val(oth), min_val=rowbase,
                                 max_val=rowbase + 1)
                    othf = flag_sh[DynSlice(o_v, 1), :]
                    s.reg_mov(cnt, 0)

                    def poll():
                        s.reg_add(cnt, cnt, 16)
                        s.dma_start(flagrd[:, :], othf).then_inc(spinsem.h, 16)
                        s.wait_ge(spinsem.h, s.compute_val(cnt))
                        s.load(flv, flagrd[0:1, 0:1].bitcast(mybir.dt.uint32))

                    poll()
                    with s.While(lambda: s.compute_val(flv) != MAGIC_U32):
                        poll()
            block.sync(body)

        cast_jobs = []
        for i in range(0, NSLAB, 6):
            cast_jobs.append(("wt", lambda g, i=i: g.dma_start(
                wT[i:min(i + 6, NSLAB), :, :],
                wtile[i:min(i + 6, NSLAB), :, :])))
        for i in range(2):
            cast_jobs.append((None, lambda g, i=i: g.dma_start(
                bnat[i * 512:(i + 1) * 512, :],
                bmat[i * 512:(i + 1) * 512, :])))
        for i in range(32):
            cast_jobs.append((None, lambda g, i=i: g.dma_start(
                woT[i, :, :], wotile[i, :, :])))
        wt_cast_tok = []

        def emit_cast():
            if cast_jobs:
                kind, job = cast_jobs.pop(0)
                GD(job)
                if kind == "wt":
                    wt_cast_tok.append(tok(qg))

        S(lambda s: s.activation(acol[:, :], acol[:, :], AF.Exp), waits=[SMALL])
        S(lambda s: s.mul(acol[:, :], acol[:, :], -1.0))


        with ExitStack() as es1:
            sb1 = SB(es1)
            es1a = ExitStack()
            sb1a = SB(es1a)
            xsb = [sb1a(f"xs{i}", [128, D]) for i in range(4)]
            xnt = [es1a.enter_context(nc.sbuf_tensor(f"xnt{i}", [128, D], BF16))
                   for i in range(4)]
            lnstat = sb1a("lnstat", [128, 8])
            lntrash = es1a.enter_context(
                nc.sbuf_tensor("lntrash", [128, D], BF16))
            xn_done_v = []
            xn_spill = []
            QG_X0 = qg.n
            for i in range(32):
                w = []
                if i >= 4:
                    w.append((vch, xn_done_v[i - 4]))
                GD(lambda g, i=i: g.dma_start(xsb[i % 4][:, :],
                                              x_in[128 * i:128 * (i + 1), :]),
                   waits=w)
                XL = tok(qg)
                S(lambda s, i=i: s.activation(lntrash[:, :], xsb[i % 4][:, :],
                                              AF.Square,
                                              accum_out=lnstat[:, 1:2]),
                  waits=[XL])
                SQ = tok(sch)
                V(lambda v, i=i: v.reduce_sum(lnstat[:, 0:1], xsb[i % 4][:, :],
                                              axis=AX.X), waits=[XL])
                V(lambda v: v.tensor_scalar(lnstat[:, 2:3], lnstat[:, 0:1],
                                            1.0 / D, None, ALU.mult))
                V(lambda v: v.tensor_tensor(lnstat[:, 4:5], lnstat[:, 2:3],
                                            lnstat[:, 2:3], ALU.mult))
                V(lambda v: v.tensor_scalar(lnstat[:, 3:4], lnstat[:, 1:2],
                                            1.0 / D, lnstat[:, 4:5],
                                            ALU.mult, ALU.subtract), waits=[SQ])
                S(lambda s: s.activation(lnstat[:, 5:6], lnstat[:, 3:4], AF.Ln,
                                         bias=epscol[:, :]), waits=[tok(vch)])
                S(lambda s: s.activation(lnstat[:, 6:7], lnstat[:, 5:6], AF.Exp,
                                         scale=-0.5))
                w2 = [tok(sch)]
                if i >= 4:
                    w2.append((qg, xn_spill[i - 4]))
                V(lambda v, i=i: v.tensor_scalar(xnt[i % 4][:, :],
                                                 xsb[i % 4][:, :],
                                                 lnstat[:, 2:3], lnstat[:, 6:7],
                                                 ALU.subtract, ALU.mult),
                  waits=w2)
                xn_done_v.append(vch.n)
                GD(lambda g, i=i: g.dma_start(xn_tok[128 * i:128 * (i + 1), :],
                                              xnt[i % 4][:, :]),
                   waits=[(vch, xn_done_v[i])])
                xn_spill.append(qg.n)
                emit_cast()
                emit_cast()
            while cast_jobs:
                emit_cast()
            LN_DONE = tok(qg)
            CASTS = LN_DONE
            WT_DONE = CASTS
            WOT_DONE = CASTS
            es1a.close()

            xnT = [es1.enter_context(nc.sbuf_tensor(f"xnT{i}", [128, TSUP], BF16))
                   for i in range(32)]
            wslab = [es1.enter_context(nc.sbuf_tensor(f"wslab{i}",
                                                      [128, 32 * FW], BF16))
                     for i in range(2)]
            evac_t = [es1.enter_context(nc.sbuf_tensor(f"evac{i}", [128, TTW],
                                                       BF16)) for i in range(2)]
            convo_t = [es1.enter_context(nc.sbuf_tensor(f"convo{i}", [128, TTW],
                                                        BF16)) for i in range(2)]
            convin = es1.enter_context(nc.sbuf_tensor("convin", [128, TTW + 3],
                                                      BF16))
            diag_t = [es1.enter_context(nc.sbuf_tensor(f"diag{i}", [128, 128],
                                                       BF16)) for i in range(4)]
            dtw = [sb1(f"dtw{i}", [64, TTW]) for i in range(2)]
            scf = sb1("scf", [64, TTW])
            sct = sb1("sct", [128, 64])
            mx1, mx2 = sb1("mx1", [128, 8]), sb1("mx2", [128, 8])
            wk1, msk = sb1("wk1", [128, 64]), sb1("msk", [128, 64])
            gex, gsc = sb1("gex", [128, 64]), sb1("gsc", [128, 2])
            g_fm = es1.enter_context(nc.sbuf_tensor("g_fm", [64, TSUP], BF16))
            h_t = es1.enter_context(nc.sbuf_tensor("h_t", [128, TTW], BF16))
            hg_t = es1.enter_context(nc.sbuf_tensor("hg_t", [128, TTW], BF16))
            dt32t = sb1("dt32t", [64, TTW])
            dtbft = es1.enter_context(nc.sbuf_tensor("dtbft", [64, TTW], BF16))

            def fb_role(fb):
                r0 = fb * 128
                if r0 < R_HS:
                    return ("gate", r0 - R_GATE)
                if r0 < R_B:
                    return ("hs", r0 - R_HS)
                if r0 < R_C:
                    return ("b", r0 - R_B)
                if r0 < R_A:
                    return ("c", r0 - R_C)
                if r0 < R_DTWR:
                    return ("a", r0 - R_A)
                if r0 < R_PAD:
                    return ("dtwr", 0)
                return ("pad", 0)

            inproj_bank_free = [None] * 4
            conv_bank_free = [None] * 2
            evac_free = [None] * 2
            convo_free = [None] * 2
            conv_mm_last = [None]
            halo_upd_last = [None]
            hg_spill_last = [None]
            mm_group = 0
            conv_group = 0
            last_super_mm = 0

            for sup in range(NSUP):
                t0 = sup * TSUP
                XNT0 = qs.n
                for dd in range(32):
                    w = [LN_DONE] if dd == 0 else []
                    if sup > 0 and dd == 0:
                        w.append((tch, last_super_mm))
                    Y(lambda s, dd=dd, t0=t0: s.dma_start_transpose(
                        xnT[dd][:, :],
                        xn_tok[t0:t0 + TSUP, 128 * dd:128 * dd + 128]), waits=w)
                XNT_OK = (qs, XNT0 + 16 * 32)

                slab_order = list(range(72)) + [80] + list(range(72, 80))
                slab_loaded = {}
                slab_consumed = {}

                def load_slab(kk, sup=sup):
                    sl = slab_order[kk]
                    w = [wt_cast_tok[sl // 6]] if sup == 0 else []
                    if sup > 0 and kk < 2:
                        w.append((tch, last_super_mm))
                    if kk >= 2:
                        w.append((tch, slab_consumed[kk - 2]))
                    GD(lambda g, sl=sl, kk=kk: g.dma_start(
                        wslab[kk % 2][:, :], wT[sl, :, :]), waits=w)
                    slab_loaded[kk] = qg.n

                load_slab(0)
                load_slab(1)

                for k, sl in enumerate(slab_order):
                    for blkk in [0]:
                        fb = sl
                        role, off = fb_role(fb)
                        if role == "pad":
                            continue
                        for tt in range(NTT):
                            gidx = mm_group
                            mm_group += 1
                            bnk = gidx % 4

                            def mm_emit(t, k=k, blkk=blkk, tt=tt, bnk=bnk):
                                for dd in range(32):
                                    ins = t.matmul(
                                        bank(bnk),
                                        wslab[k % 2][:, FW * dd + 128 * blkk:
                                                     FW * dd + 128 * blkk + 128],
                                        xnT[dd][:, TTW * tt:TTW * (tt + 1)],
                                        start=(dd == 0), stop=(dd == 31))
                                return ins

                            T(mm_emit, waits=[(qg, slab_loaded[k]), XNT_OK,
                                              inproj_bank_free[bnk]])
                            MM_OK = tok(tch)

                            if role == "gate":
                                ev = gidx % 2
                                S(lambda s, bnk=bnk, ev=ev: s.activation(
                                    evac_t[ev][:, :], bank(bnk), AF.Silu),
                                  waits=[MM_OK, evac_free[ev]])
                                inproj_bank_free[bnk] = tok(sch)
                                GD(lambda g, off=off, tt=tt, t0=t0, ev=ev:
                                   g.dma_start(
                                       sg_s[off:off + 128,
                                            t0 + TTW * tt:t0 + TTW * (tt + 1)],
                                       evac_t[ev][:, :]), waits=[tok(sch)])
                                evac_free[ev] = tok(qg)
                            elif role in ("hs", "b", "c"):
                                cfb = ({"hs": 0, "b": 32, "c": 36}[role]
                                       + off // 128)
                                dst = {"hs": hs_s, "b": bx_s, "c": cx_s}[role]
                                S(lambda s, bnk=bnk: s.activation(
                                    convin[:, 3:3 + TTW], bank(bnk),
                                    AF.Identity),
                                  waits=[MM_OK, conv_mm_last[0],
                                         halo_upd_last[0]])
                                inproj_bank_free[bnk] = tok(sch)
                                EVC = tok(sch)
                                V(lambda v, cfb=cfb: v.tensor_copy(
                                    convin[:, 0:3],
                                    halo[:, 3 * cfb:3 * cfb + 3]), waits=[EVC])
                                if tt == 0:
                                    for kk in range(4):
                                        V(lambda v, cfb=cfb, kk=kk:
                                          v.tensor_scalar(
                                              diag_t[kk][:, :], i128b[:, :],
                                              cw_t[:, 4 * cfb + kk:
                                                   4 * cfb + kk + 1],
                                              None, ALU.mult),
                                          waits=[conv_mm_last[0]])
                                CIN = tok(vch)
                                cgidx = conv_group
                                conv_group += 1
                                cbnk = 4 + cgidx % 2

                                def conv_emit(t, cbnk=cbnk):
                                    for kk in range(4):
                                        ins = t.matmul(
                                            bank(cbnk), diag_t[kk][:, :],
                                            convin[:, kk:kk + TTW],
                                            start=(kk == 0), stop=(kk == 3))
                                    return ins

                                T(conv_emit,
                                  waits=[CIN, conv_bank_free[cgidx % 2]])
                                conv_mm_last[0] = tok(tch)
                                CONV_OK = tok(tch)
                                V(lambda v, cfb=cfb: v.tensor_copy(
                                    halo[:, 3 * cfb:3 * cfb + 3],
                                    convin[:, TTW:TTW + 3]), waits=[CONV_OK])
                                halo_upd_last[0] = tok(vch)
                                co = cgidx % 2
                                S(lambda s, cbnk=cbnk, cfb=cfb, co=co:
                                  s.activation(convo_t[co][:, :], bank(cbnk),
                                               AF.Silu,
                                               bias=cb_t[:, cfb:cfb + 1]),
                                  waits=[CONV_OK, convo_free[co]])
                                conv_bank_free[cgidx % 2] = tok(sch)
                                GD(lambda g, dst=dst, off=off, tt=tt, t0=t0,
                                   co=co: g.dma_start(
                                       dst[off:off + 128,
                                           t0 + TTW * tt:t0 + TTW * (tt + 1)],
                                       convo_t[co][:, :]), waits=[tok(sch)])
                                convo_free[co] = tok(qg)
                            elif role == "dtwr":
                                S(lambda s, bnk=bnk: s.activation(
                                    dtw[0][:, :], bank(bnk)[0:64, :], AF.Abs,
                                    bias=dtbias_t[:, :]), waits=[MM_OK])
                                S(lambda s: s.activation(dtw[1][:, :],
                                                         dtw[0][:, :],
                                                         AF.Exp, scale=-1.0))
                                S(lambda s: s.activation(dtw[0][:, :],
                                                         dtw[1][:, :],
                                                         AF.Ln,
                                                         bias=onecol[0:64, :]))
                                S(lambda s, bnk=bnk: s.activation(
                                    dtw[1][:, :], bank(bnk)[0:64, :], AF.Relu,
                                    bias=dtbias_t[:, :]))
                                S(lambda s, bnk=bnk: s.activation(
                                    scf[:, :], bank(bnk)[64:128, :],
                                    AF.Identity))
                                inproj_bank_free[bnk] = tok(sch)
                                V(lambda v: v.tensor_add(dt32t[:, :],
                                                         dtw[0][:, :],
                                                         dtw[1][:, :]),
                                  waits=[tok(sch)])
                                V(lambda v: v.tensor_copy(dtbft[:, :],
                                                          dt32t[:, :]))
                                DTV = tok(vch)
                                GD(lambda g, tt=tt, t0=t0: g.dma_start(
                                    dt32_s[:, t0 + TTW * tt:
                                           t0 + TTW * (tt + 1)], dt32t[:, :]),
                                   waits=[DTV])
                                GD(lambda g, tt=tt, t0=t0: g.dma_start(
                                    dtb_s[:, t0 + TTW * tt:
                                          t0 + TTW * (tt + 1)], dtbft[:, :]),
                                   waits=[DTV])

                                for q in range(4):
                                    T(lambda t, q=q: t.transpose(
                                        bank(6)[:, 0:64],
                                        scf[:, 128 * q:128 * (q + 1)],
                                        i128f[0:64, 0:64]),
                                      waits=[tok(sch), tok(vch)])
                                    S(lambda s: s.activation(
                                        sct[:, :], bank(6)[:, 0:64],
                                        AF.Identity), waits=[tok(tch)])
                                    V(lambda v: v.max(out=mx1[:, :],
                                                      in_=sct[:, :]),
                                      waits=[tok(sch)])
                                    V(lambda v: v.match_replace(
                                        out=wk1[:, :], in_to_replace=mx1[:, :],
                                        in_values=sct[:, :], imm_value=-1e30))
                                    V(lambda v: v.max(out=mx2[:, :],
                                                      in_=wk1[:, :]))
                                    V(lambda v: v.tensor_single_scalar(
                                        msk[:, :], sct[:, :], mx2[:, 7:8],
                                        ALU.is_ge))
                                    V(lambda v: v.tensor_scalar(
                                        gsc[:, 0:1], mx1[:, 0:1], -1.0, None,
                                        ALU.mult))
                                    S(lambda s: s.activation(
                                        gex[:, :], sct[:, :], AF.Exp,
                                        bias=gsc[:, 0:1]), waits=[tok(vch)])
                                    V(lambda v: v.tensor_mul(
                                        gex[:, :], gex[:, :], msk[:, :]),
                                      waits=[tok(sch)])
                                    V(lambda v: v.reduce_sum(
                                        gsc[:, 1:2], gex[:, :], axis=AX.X))
                                    V(lambda v: v.reciprocal(gsc[:, 1:2],
                                                             gsc[:, 1:2]))
                                    V(lambda v: v.tensor_scalar(
                                        gex[:, :], gex[:, :], gsc[:, 1:2],
                                        None, ALU.mult))
                                    T(lambda t: t.transpose(
                                        bank(6)[0:64, 64:192], gex[:, :],
                                        i128f[:, :]),
                                      waits=[tok(vch), tok(sch)])
                                    S(lambda s, q=q, tt=tt: s.activation(
                                        g_fm[0:64, TTW * tt + 128 * q:
                                             TTW * tt + 128 * (q + 1)],
                                        bank(6)[0:64, 64:192], AF.Identity),
                                      waits=[tok(tch)])
                            elif role == "a":
                                bkr = off // 128
                                S(lambda s, bnk=bnk: s.activation(
                                    h_t[:, :], bank(bnk), AF.Identity),
                                  waits=[MM_OK, hg_spill_last[0]])
                                inproj_bank_free[bnk] = tok(sch)
                                H_OK = tok(sch)
                                T(lambda t, bkr=bkr, tt=tt: t.matmul(
                                    bank(6),
                                    selt[:, 128 * bkr:128 * (bkr + 1)],
                                    g_fm[0:64, TTW * tt:TTW * (tt + 1)],
                                    start=True, stop=True),
                                  waits=[H_OK, tok(vch)])
                                V(lambda v: v.tensor_mul(hg_t[:, :], h_t[:, :],
                                                         bank(6)),
                                  waits=[tok(tch)])
                                GD(lambda g, bkr=bkr, tt=tt, sup=sup:
                                   g.dma_start(
                                       hgq[sup, :, bkr,
                                           TTW * tt:TTW * (tt + 1)],
                                       hg_t[:, :]), waits=[tok(vch)])
                                hg_spill_last[0] = tok(qg)
                    slab_consumed[k] = tch.n
                    if k + 2 < len(slab_order):
                        load_slab(k + 2)
                last_super_mm = tch.n
            P1_SPILLS = tok(qg)
            P1_V = tok(vch)
            P1_S = tok(sch)
            P1_T = tok(tch)

        with ExitStack() as es2:
            sb2 = SB(es2)
            hs_tok = [es2.enter_context(nc.sbuf_tensor(f"hs_tok{i}",
                                                       [128, 4096], BF16))
                      for i in range(2)]
            sg_tok = [es2.enter_context(nc.sbuf_tensor(f"sg_tok{i}",
                                                       [128, 4096], BF16))
                      for i in range(2)]
            dttokb = [es2.enter_context(nc.sbuf_tensor(f"dttokb{i}", [128, 64],
                                                       BF16)) for i in range(2)]
            dtfm = [sb2(f"dtfm{i}", [64, 128]) for i in range(2)]
            bfm = [es2.enter_context(nc.sbuf_tensor(f"bfm{i}", [128, 1024],
                                                    BF16)) for i in range(2)]
            cfm = [es2.enter_context(nc.sbuf_tensor(f"cfm{i}", [128, 1024],
                                                    BF16)) for i in range(2)]
            btok = [es2.enter_context(nc.sbuf_tensor(f"btok{i}", [128, 1024],
                                                     BF16)) for i in range(2)]
            arena = [sb2(f"arena{i}", [128, 1024]) for i in range(2)]
            lhsT_a = [sb2(f"lhsTa{i}", [128, 128]) for i in range(2)]
            sstate = sb2("sstate", [128, NH * HP])
            sstate_bf = es2.enter_context(nc.sbuf_tensor("sstate_bf",
                                                         [128, NH * HP], BF16))
            acs_fm = sb2("acs_fm", [64, 128])
            adt_fm = sb2("adt_fm", [64, 128])
            acs_tok = sb2("acs_tok", [128, 64])
            lndt = sb2("lndt", [128, 64])
            c4t = sb2("c4t", [128, 64])
            a_ct = sb2("a_ct", [1, 64])
            d2t = sb2("d2t", [128, 64])
            decayt = es2.enter_context(nc.sbuf_tensor("decayt", [128, 64], BF16))
            sdot = sb2("sdot", [128, 64])
            expat = sb2("expat", [128, 64])
            dtdec = es2.enter_context(nc.sbuf_tensor("dtdec", [128, 64], BF16))
            drest = es2.enter_context(nc.sbuf_tensor("drest", [128, 4096], BF16))
            xhatt = es2.enter_context(nc.sbuf_tensor("xhatt", [128, 4096], BF16))
            gm_m = es2.enter_context(nc.sbuf_tensor("gm_m", [128, 128], BF16))
            esb = [es2.enter_context(nc.sbuf_tensor(f"esb{i}", [128, 512], BF16))
                   for i in range(2)]
            eexp = [es2.enter_context(nc.sbuf_tensor(f"eexp{i}", [128, 512],
                                                     BF16)) for i in range(2)]
            mslab = [es2.enter_context(nc.sbuf_tensor(f"mslab{i}", [128, 512],
                                                      BF16)) for i in range(2)]
            ytmp = sb2("ytmp", [128, 512])
            ybf = es2.enter_context(nc.sbuf_tensor("ybf", [128, 4096], BF16))
            xgt = es2.enter_context(nc.sbuf_tensor("xgt", [128, 4096], BF16))
            nx1 = es2.enter_context(nc.sbuf_tensor("nx1", [128, 4096], BF16))
            nrmt = es2.enter_context(nc.sbuf_tensor("nrmt", [128, 4096], BF16))
            ssqt = sb2("ssqt", [128, 8])
            sqtrash = es2.enter_context(nc.sbuf_tensor("sqtrash", [128, 1024],
                                                       BF16))

            upper = sb2("upper", [128, 128])
            d_rows = es2.enter_context(nc.sbuf_tensor("d_rows", [128, NH],
                                                      BF16))
            nw_rows = es2.enter_context(nc.sbuf_tensor("nw_rows",
                                                       [128, NH * HP], BF16))
            GD(lambda g: g.dma_start(upper[:, :], c_upper))
            T(lambda t: t.matmul(PS[:, 0:NH], onesrow[0:1, 0:128],
                                 dvec_row[0:1, :], start=True, stop=True),
              waits=[SMALL, CONSTS_V, P1_T])
            V(lambda v: v.tensor_copy(d_rows[:, :], PS[:, 0:NH]),
              waits=[tok(tch), P1_V])
            for j in range(8):
                T(lambda t, j=j: t.matmul(bank(1), onesrow[0:1, 0:128],
                                          nwrow[0:1, 512 * j:512 * (j + 1)],
                                          start=True, stop=True),
                  waits=[tok(vch)])
                V(lambda v, j=j: v.tensor_copy(
                    nw_rows[:, 512 * j:512 * (j + 1)], bank(1)),
                  waits=[tok(tch)])
            V(lambda v: v.memset(sstate[:, :], 0.0))
            V(lambda v: v.memset(sstate_bf[:, :], 0.0))
            V(lambda v: v.memset(arena[0][:, :], 0.0))
            V(lambda v: v.memset(arena[1][:, :], 0.0))
            V(lambda v: v.memset(lhsT_a[0][:, :], 0.0))
            V(lambda v: v.memset(lhsT_a[1][:, :], 0.0))
            AR_Z = tok(vch)
            GD(lambda g: g.dma_start(arena[0][0:4, :], c_delta[:, 0:1024]),
               waits=[AR_Z])
            GD(lambda g: g.dma_start(arena[1][0:4, :], c_delta[:, 0:1024]))
            GD(lambda g: g.dma_start(lhsT_a[0][4:5, 0:128],
                                     onesrow[0:1, 0:128]))
            GD(lambda g: g.dma_start(lhsT_a[1][4:5, 0:128],
                                     onesrow[0:1, 0:128]))
            ARENA_INIT = tok(qg)

            GD(lambda g: g.dma_start(
                bxcx_sh[DynSlice(par(g, "g") * 512, 512), :], bx_s),
               waits=[P1_SPILLS])
            GD(lambda g: g.dma_start(
                bxcx_sh[DynSlice(1024 + par(g, "g") * 512, 512), :], cx_s))
            BXCX_W = tok(qg)
            GD(lambda g: g.dma_start(
                flag_sh[DynSlice(par(g, "g") + 2, 1), :], magic_t[:, :]),
               waits=[BXCX_W, FLAG_CLEARED])
            emit_spin(2, qsp2, "sp2")

            arena_free = [None, None]
            ebank_free = [None, None]
            mslab_free = [None, None]
            mslab_ok = [None, None]
            ydbank_free = [None, None]
            eexp_s = [None, None]
            yobank_free = [None, None]
            stbank_free = [None, None]
            sbf_ok = [None] * 8
            chunk_end_v = {}
            chunk_end_t = {}

            for c in range(NCH):
                p2 = c % 2
                ct0 = c * 128
                cw = [BXCX_W] if c == 0 else []
                if c >= 2:
                    cw = [(vch, chunk_end_v[c - 2]), (tch, chunk_end_t[c - 2])]
                Y(lambda s: s.dma_start_transpose(hs_tok[p2][:, :],
                                                  hs_s[:, ct0:ct0 + 128]),
                  waits=cw)
                Y(lambda s: s.dma_start_transpose(sg_tok[p2][:, :],
                                                  sg_s[:, ct0:ct0 + 128]))
                Y(lambda s: s.dma_start_transpose(dttokb[p2][:, :],
                                                  dtb_s[:, ct0:ct0 + 128]))
                Y(lambda s: s.dma_start_transpose(btok[p2][:, :],
                                                  bxcx_sh[0:1024,
                                                          ct0:ct0 + 128]))
                Y(lambda s: s.dma_start(dtfm[p2][:, :],
                                        dt32_s[:, ct0:ct0 + 128]))
                Y(lambda s: s.dma_start(
                    bfm[p2][:, :].rearrange("p (g t) -> p g t", g=8),
                    bxcx_sh[0:1024, ct0:ct0 + 128].rearrange(
                        "(g p) t -> p g t", p=128)))
                Y(lambda s: s.dma_start(
                    cfm[p2][:, :].rearrange("p (g t) -> p g t", g=8),
                    bxcx_sh[1024:2048, ct0:ct0 + 128].rearrange(
                        "(g p) t -> p g t", p=128)))
                LD = tok(qs)

                V(lambda v: v.tensor_scalar(adt_fm[:, :], dtfm[p2][:, :],
                                            acol[:, :], None, ALU.mult),
                  waits=[LD])
                V(lambda v: v.tensor_tensor_scan(acs_fm[:, :], adt_fm[:, :],
                                                 zero64[:, :], 0.0,
                                                 ALU.add, ALU.add))
                ACSFM_V = tok(vch)
                GD(lambda g: g.dma_start(acsflat, acs_fm[:, :]),
                   waits=[ACSFM_V, arena_free[0], arena_free[1]])
                ACSFLAT_W = tok(qg)
                T(lambda t: t.transpose(bank(1)[:, 0:64], acs_fm[:, :],
                                        i128f[0:64, 0:64]),
                  waits=[ACSFM_V, ydbank_free[1]])
                S(lambda s: s.activation(acs_tok[:, :], bank(1)[:, 0:64],
                                         AF.Identity), waits=[tok(tch)])
                ACS_S = tok(sch)
                S(lambda s: s.activation(lndt[:, :], dttokb[p2][:, :], AF.Ln),
                  waits=[LD])
                V(lambda v: v.tensor_sub(c4t[:, :], acs_tok[:, :],
                                         lndt[:, :]), waits=[tok(sch)])
                C4_V = tok(vch)
                T(lambda t: t.matmul(bank(1)[0:1, 64:128], e127[:, :],
                                     acs_tok[:, :], start=True, stop=True),
                  waits=[ACS_S])
                S(lambda s: s.activation(a_ct[:, :], bank(1)[0:1, 64:128],
                                         AF.Identity), waits=[tok(tch)])
                T(lambda t: t.matmul(bank(1)[:, 128:192], onesrow[0:1, 0:128],
                                     a_ct[0:1, :], start=True, stop=True),
                  waits=[tok(sch)])
                V(lambda v: v.tensor_sub(d2t[:, :], bank(1)[:, 128:192],
                                         acs_tok[:, :]), waits=[tok(tch)])
                D2_V = tok(vch)
                S(lambda s: s.activation(decayt[:, :], d2t[:, :], AF.Exp),
                  waits=[D2_V])
                S(lambda s: s.activation(sdot[:, :], acs_tok[:, :], AF.Exp))
                S(lambda s: s.activation(expat[:, :], bank(1)[:, 128:192],
                                         AF.Exp))
                EXPA_S = tok(sch)
                V(lambda v: v.tensor_mul(dtdec[:, :], dttokb[p2][:, :],
                                         decayt[:, :]), waits=[tok(sch)])
                V(lambda v: v.tensor_tensor(
                    drest[:, :].rearrange("t (h p) -> t h p", p=HP),
                    hs_tok[p2][:, :].rearrange("t (h p) -> t h p", p=HP),
                    d_rows[:, :].unsqueeze(2).broadcast_to([128, NH, HP]),
                    ALU.mult))
                V(lambda v: v.tensor_tensor(
                    xhatt[:, :].rearrange("t (h p) -> t h p", p=HP),
                    hs_tok[p2][:, :].rearrange("t (h p) -> t h p", p=HP),
                    dtdec[:, :].unsqueeze(2).broadcast_to([128, NH, HP]),
                    ALU.mult))
                XHAT_V = tok(vch)

                for sg in range(8):
                    sgp = sg % 2
                    ydb = bank(0 + sgp)
                    yob = bank(2 + sgp)
                    stb = bank(4 + sgp)

                    def gview(ap4096):
                        return ap4096.rearrange(
                            "t (k g2 p) -> t g2 k p", g2=8, p=HP)[:, sg, :, :]

                    T(lambda t, sg=sg: t.matmul(
                        yob[:, 0:128],
                        bfm[p2][:, 128 * sg:128 * (sg + 1)],
                        cfm[p2][:, 128 * sg:128 * (sg + 1)],
                        start=True, stop=True),
                      waits=[LD, yobank_free[sgp]])
                    V(lambda v, sg=sg: v.tensor_tensor(
                        gm_m[:, :], yob[:, 0:128],
                        upper[:, :], ALU.mult), waits=[tok(tch), tok(pch)])
                    GM_V = tok(vch)

                    GD(lambda g, sg=sg: g.dma_start(
                        arena[sgp][4:5, :].rearrange("o (k t) -> o k t", k=8),
                        acsflat.rearrange("(k g2) t -> g2 k t", g2=8)[
                            sg:sg + 1, :, :].squeeze(0).unsqueeze(0)),
                       waits=[ACSFLAT_W, ARENA_INIT, arena_free[sgp]])
                    AR_OK = tok(qg)

                    for si in range(2):
                        j = 2 * sg + si
                        T(lambda t, sg=sg, si=si: t.transpose(
                            stb[0:4, 384:512],
                            c4t[:, :].rearrange("t (k g2) -> t g2 k", g2=8)[
                                :, sg, 4 * si:4 * si + 4],
                            i128f[:, :]),
                          waits=[C4_V, EXPA_S, D2_V, tok(vch),
                                 stbank_free[sgp]])
                        V(lambda v: v.tensor_copy(
                            lhsT_a[j % 2][0:4, :], stb[0:4, 384:512]),
                          waits=[tok(tch)])
                        LH_V = tok(vch)
                        T(lambda t, j=j, sgp=sgp, si=si: t.matmul(
                            bank(6 + j % 2), lhsT_a[j % 2][:, :],
                            arena[sgp][:, 512 * si:512 * (si + 1)],
                            start=True, stop=True),
                          waits=[LH_V, AR_OK, ebank_free[j % 2]])
                        EMM = tok(tch)
                        if si == 1:
                            arena_free[sgp] = EMM
                        V(lambda v, j=j: v.tensor_scalar(
                            esb[j % 2][:, :], bank(6 + j % 2), 30.0, None,
                            ALU.min), waits=[EMM, eexp_s[j % 2]])
                        ebank_free[j % 2] = tok(vch)
                        S(lambda s, j=j: s.activation(
                            eexp[j % 2][:, :], esb[j % 2][:, :], AF.Exp),
                          waits=[tok(vch), tok(pch)])
                        eexp_s[j % 2] = tok(sch)
                        P(lambda p, j=j, si=si: p.tensor_tensor(
                            mslab[si][:, :].rearrange("s (a l) -> s a l", a=4),
                            eexp[j % 2][:, :].rearrange("s (a l) -> s a l",
                                                        a=4),
                            gm_m[:, :].unsqueeze(1)
                            .broadcast_to([128, 4, 128]), ALU.mult),
                          waits=[tok(sch), GM_V, mslab_free[si]])
                        mslab_ok[si] = tok(pch)

                    T(lambda t, sg=sg: t.matmul(
                        ydb, i128b[:, :], gview(drest[:, :]),
                        start=True, stop=True),
                      waits=[XHAT_V, ydbank_free[sgp], EXPA_S, ACS_S])

                    def ydiag_emit(t, sg=sg):
                        for k in range(8):
                            h = sg + 8 * k
                            ins = t.matmul(
                                ydb[:, 64 * k:64 * (k + 1)],
                                mslab[k // 4][:, 128 * (k % 4):
                                              128 * (k % 4 + 1)],
                                hs_tok[p2][:, 64 * h:64 * (h + 1)],
                                start=False, stop=True)
                        return ins
                    T(ydiag_emit, waits=[mslab_ok[0], mslab_ok[1]])
                    YD = tok(tch)
                    mslab_free[0] = YD
                    mslab_free[1] = YD

                    T(lambda t, sg=sg: t.matmul(
                        yob, cfm[p2][:, 128 * sg:128 * (sg + 1)],
                        gview(sstate_bf[:, :]),
                        start=True, stop=True),
                      waits=[LD, yobank_free[sgp], sbf_ok[sg], GM_V])
                    YO = tok(tch)

                    T(lambda t, sg=sg: t.matmul(
                        stb, btok[p2][:, 128 * sg:128 * (sg + 1)],
                        gview(xhatt[:, :]),
                        start=True, stop=True),
                      waits=[XHAT_V, LD, tok(vch), stbank_free[sgp]])
                    ST = tok(tch)

                    V(lambda v, sg=sg: v.tensor_tensor(
                        ytmp[:, :].rearrange("t (a p) -> t a p", p=HP),
                        yob.rearrange("t (a p) -> t a p", p=HP),
                        sdot[:, :].rearrange("t (k g2) -> t g2 k", g2=8)[
                            :, sg, :].unsqueeze(2)
                        .broadcast_to([128, 8, HP]), ALU.mult), waits=[YO])
                    yobank_free[sgp] = tok(vch)
                    V(lambda v, sg=sg: v.tensor_tensor(
                        gview(ybf[:, :]),
                        ytmp[:, :].rearrange("t (a p) -> t a p", p=HP),
                        ydb.rearrange("t (a p) -> t a p", p=HP),
                        ALU.add), waits=[YD])
                    ydbank_free[sgp] = tok(vch)
                    V(lambda v, sg=sg: v.tensor_tensor(
                        gview(sstate[:, :]), gview(sstate[:, :]),
                        expat[:, :].rearrange("t (k g2) -> t g2 k", g2=8)[
                            :, sg, :].unsqueeze(2)
                        .broadcast_to([128, 8, HP]), ALU.mult),
                      waits=[EXPA_S])
                    V(lambda v, sg=sg: v.tensor_tensor(
                        gview(sstate[:, :]), gview(sstate[:, :]),
                        stb.rearrange("n (a p) -> n a p", p=HP),
                        ALU.add), waits=[ST])
                    stbank_free[sgp] = tok(vch)
                    V(lambda v, sg=sg: v.tensor_copy(
                        gview(sstate_bf[:, :]), gview(sstate[:, :])))
                    sbf_ok[sg] = tok(vch)

                V(lambda v: v.tensor_mul(xgt[:, :], ybf[:, :],
                                         sg_tok[p2][:, :]))
                XG_V = tok(vch)
                for g4 in range(4):
                    S(lambda s, g4=g4: s.activation(
                        sqtrash[:, :], xgt[:, 1024 * g4:1024 * (g4 + 1)],
                        AF.Square, accum_out=ssqt[:, g4:g4 + 1]),
                      waits=[XG_V])
                S(lambda s: s.activation(ssqt[:, 4:8], ssqt[:, 0:4], AF.Ln,
                                         scale=1.0 / 1024.0,
                                         bias=epscol[:, :]))
                S(lambda s: s.activation(ssqt[:, 0:4], ssqt[:, 4:8], AF.Exp,
                                         scale=-0.5))
                V(lambda v: v.tensor_tensor(
                    nx1[:, :].rearrange("t (a f) -> t a f", a=4),
                    xgt[:, :].rearrange("t (a f) -> t a f", a=4),
                    ssqt[:, 0:4].unsqueeze(2).broadcast_to([128, 4, 1024]),
                    ALU.mult), waits=[tok(sch)])
                V(lambda v: v.tensor_mul(nrmt[:, :], nx1[:, :], nw_rows[:, :]))
                NRM_V = tok(vch)
                GD(lambda g, c=c: g.dma_start(
                    normed_loc[128 * c:128 * (c + 1), :],
                    nrmt[:, :]), waits=[NRM_V])
                chunk_end_v[c] = vch.n
                chunk_end_t[c] = tch.n
            P2_SPILLS = tok(qg)
            P2_T = tok(tch)
            P2_V = tok(vch)
            P2_P = tok(pch)
            P2_S2 = tok(sch)

        GD(lambda g: g.dma_start(
            normed_sh[:, DynSlice(par(g, "g") * 4096, 4096)],
            normed_loc), waits=[P2_SPILLS])
        NORMED_SH_W = tok(qg)
        GD(lambda g: g.dma_start(flag_sh[parity["g"], :].unsqueeze(0),
                                 magic_t[:, :]),
           waits=[NORMED_SH_W, FLAG_CLEARED])

        with ExitStack() as esf:
            hgt3 = esf.enter_context(nc.sbuf_tensor("hgt3", [128, 8, 2048],
                                                    BF16))
            bds = [esf.enter_context(nc.sbuf_tensor(f"bds{i}", [128, 8192],
                                                    BF16)) for i in range(2)]
            ff_t = [esf.enter_context(nc.sbuf_tensor(f"ff_t{i}", [128, 1024],
                                                     BF16)) for i in range(2)]
            Y2(lambda s: s.dma_start(
                hgt3[:, :, :],
                hgq[DynSlice(par(s, "s"), 1), :, :, :].squeeze(0)),
               waits=[P2_SPILLS, P2_T, P2_V, P2_P, P2_S2])
            Y2(lambda s: s.dma_start(
                x_loc, x_in[DynSlice(par(s, "s") * 2048, 2048), :]))
            HGT_OK = tok(qs2)

            bnr = bnat.rearrange("(kr p) c -> p kr c", p=128)
            bds_tok = [None] * 4
            ffgrp_end = {}

            def load_bds(ds):
                w = ([CASTS, P2_SPILLS, P2_T, P2_V, P2_P]
                     if ds < 2 else [(tch, ffgrp_end[ds - 2])])
                SD(lambda s, ds=ds: s.dma_start(
                    bds[ds % 2][:, :].rearrange("p (kr c) -> p kr c", c=1024),
                    bnr[:, :, 1024 * ds:1024 * (ds + 1)]), waits=w)
                bds_tok[ds] = tok(qa)

            load_bds(0)
            load_bds(1)
            ffbank_free = [None, None]
            ff_free = [None, None]
            for ds in range(4):
                for tc in range(16):
                    idx = ds * 16 + tc
                    pb = 4 + 2 * (idx % 2)

                    def ffb_emit(t, ds=ds, tc=tc, pb=pb):
                        for kr in range(8):
                            for hf in range(2):
                                ins = t.matmul(
                                    bank(pb + hf),
                                    hgt3[:, kr, 128 * tc:128 * (tc + 1)],
                                    bds[ds % 2][:, 1024 * kr + 512 * hf:
                                                1024 * kr + 512 * (hf + 1)],
                                    start=(kr == 0), stop=(kr == 7))
                        return ins
                    wf = [HGT_OK, bds_tok[ds], ffbank_free[idx % 2]]
                    if idx == 0:
                        wf.append(P2_V)
                    T(ffb_emit, waits=wf)
                    FMM = tok(tch)
                    S(lambda s, pb=pb, idx=idx: s.activation(
                        ff_t[idx % 2][:, :], bank(pb, 2), AF.Identity,
                        scale=2.0), waits=[FMM, ff_free[idx % 2]])
                    ffbank_free[idx % 2] = tok(sch)
                    SD(lambda s, ds=ds, tc=tc, idx=idx: s.dma_start(
                        ffn_s[128 * tc:128 * (tc + 1),
                              1024 * ds:1024 * (ds + 1)],
                        ff_t[idx % 2][:, :]))
                    ff_free[idx % 2] = tok(qa)
                ffgrp_end[ds] = tch.n
                if ds + 2 < 4:
                    load_bds(ds + 2)
            FFN_SPILLS = tok(qa)
            FFN_T = tok(tch)

        emit_spin(0, qsp, "sp1")
        Y2(lambda s: s.dma_start(
            normed_loc3,
            normed_sh[DynSlice(par(s, "s") * 2048, 2048), :]),
           waits=[NORMED_SH_W])
        BULK_OK = tok(qs2)

        with ExitStack() as es3:
            sb3 = SB(es3)
            nrm3 = [es3.enter_context(nc.sbuf_tensor(f"nrm3_{i}", [128, 1024],
                                                     BF16)) for i in range(64)]
            es3w = ExitStack()
            wp3 = [es3w.enter_context(nc.sbuf_tensor(f"wp3_{i}", [128, 8192],
                                                     BF16)) for i in range(2)]
            at_t = [es3w.enter_context(nc.sbuf_tensor(f"at_t{i}", [128, 1024],
                                                      BF16)) for i in range(2)]
            p3bank_free = [None, None]
            at_free = [None, None]
            wp3_used = []
            p3_tile_done = []
            for t3 in range(2):
                NRM0 = qs2.n
                for ic in range(64):
                    w = []
                    if t3 == 0 and ic == 0:
                        w = [BULK_OK, FFN_T, FFN_SPILLS, P2_V, P2_P]
                    if t3 > 0 and ic == 0:
                        w = [(tch, p3_tile_done[t3 - 1])]
                    Y2(lambda s, t3=t3, ic=ic: s.dma_start_transpose(
                        nrm3[ic][:, :],
                        normed_loc3[1024 * t3:1024 * (t3 + 1),
                                    128 * ic:128 * (ic + 1)]), waits=w)
                NRM_OK = (qs2, NRM0 + 16 * 64)
                for og in range(32):
                    gi = t3 * 32 + og
                    w = [WOT_DONE, FFN_T, FFN_SPILLS] if gi == 0 else []
                    if gi >= 2:
                        w.append((tch, wp3_used[gi - 2]))
                    GD(lambda g, og=og, gi=gi: g.dma_start(
                        wp3[gi % 2][:, :], woT[og, :, :]), waits=w)
                    WP3 = tok(qg)
                    for half in range(2):
                        bnk2 = 2 * (gi % 2) + half

                        def p3_emit(t, gi=gi, half=half, bnk2=bnk2):
                            for ic in range(64):
                                ins = t.matmul(
                                    bank(bnk2),
                                    wp3[gi % 2][:, 128 * ic:128 * (ic + 1)],
                                    nrm3[ic][:, 512 * half:512 * (half + 1)],
                                    start=(ic == 0), stop=(ic == 63))
                            return ins
                        T(p3_emit,
                          waits=([WP3, NRM_OK, p3bank_free[gi % 2]]
                                 if half == 0 else []))
                    wp3_used.append(tch.n)
                    PMM = tok(tch)
                    S(lambda s, gi=gi: s.activation(
                        at_t[gi % 2][:, :], bank(2 * (gi % 2), 2),
                        AF.Identity), waits=[PMM, at_free[gi % 2]])
                    p3bank_free[gi % 2] = tok(sch)
                    GD(lambda g, og=og, t3=t3, gi=gi: g.dma_start(
                        attn_s[128 * og:128 * (og + 1),
                               1024 * t3:1024 * (t3 + 1)],
                        at_t[gi % 2][:, :]), waits=[tok(sch)])
                    at_free[gi % 2] = tok(qg)
                p3_tile_done.append(tch.n)
            P3_SPILLS = tok(qg)
            P3_T = tok(tch)
            P3_S = tok(sch)
            es3w.close()

            x4 = [sb3(f"x4_{i}", [128, 1024]) for i in range(4)]
            o4 = [sb3(f"o4_{i}", [128, 1024]) for i in range(4)]
            at4 = [es3.enter_context(nc.sbuf_tensor(f"at4_{i}", [128, 1024],
                                                    BF16)) for i in range(4)]
            ff4 = [es3.enter_context(nc.sbuf_tensor(f"ff4_{i}", [128, 1024],
                                                    BF16)) for i in range(4)]
            p4_done = []
            p4_spill = []
            for ii in range(64):
                tc, dch = divmod(ii, 4)
                q4 = ii % 4
                w = []
                if ii == 0:
                    w = [P3_T, P3_SPILLS, P3_S, FFN_SPILLS]
                if ii >= 4:
                    w = [(vch, p4_done[ii - 4])]
                Y2(lambda s, tc=tc, dch=dch, q4=q4: s.dma_start(
                    x4[q4][:, :],
                    x_loc[128 * tc:128 * (tc + 1),
                          1024 * dch:1024 * (dch + 1)]), waits=w)
                Y2(lambda s, tc=tc, dch=dch, q4=q4: s.dma_start_transpose(
                    at4[q4][:, :],
                    attn_s[1024 * dch:1024 * (dch + 1),
                           128 * tc:128 * (tc + 1)]))
                Y2(lambda s, tc=tc, dch=dch, q4=q4: s.dma_start(
                    ff4[q4][:, :],
                    ffn_s[128 * tc:128 * (tc + 1),
                          1024 * dch:1024 * (dch + 1)]))
                L4 = tok(qs2)
                w5 = [L4]
                if ii >= 4:
                    w5.append((qa, p4_spill[ii - 4]))
                V(lambda v, q4=q4: v.tensor_add(o4[q4][:, :], x4[q4][:, :],
                                                at4[q4][:, :]), waits=w5)
                V(lambda v, q4=q4: v.tensor_add(o4[q4][:, :], o4[q4][:, :],
                                                ff4[q4][:, :]))
                p4_done.append(vch.n)
                SD(lambda s, tc=tc, dch=dch, q4=q4: s.dma_start(
                    out[128 * tc:128 * (tc + 1),
                        1024 * dch:1024 * (dch + 1)], o4[q4][:, :]),
                   waits=[tok(vch)])
                p4_spill.append(qa.n)
            OUT_DONE = tok(qa)

        for dap, src in dumps:
            GD(lambda g, dap=dap, src=src: g.dma_start(dap, src),
               waits=[OUT_DONE])
        DUMP_DONE = tok(qg)

        @block.sync
        def _(s):
            s.wait_ge(qa.h, qa.n)
            s.wait_ge(qg.h, DUMP_DONE[1])
            s.wait_ge(qs2.h, qs2.n)
    return nc


def make_consts():
    i128 = np.eye(128, dtype=np.float32)
    upper01 = (np.arange(128)[None, :] >= np.arange(128)[:, None]).astype(
        np.float32)
    delta = np.zeros((4, 2048), np.float32)
    for sl in range(4):
        for k in range(4):
            delta[k, 512 * sl + 128 * k:512 * sl + 128 * (k + 1)] = -1.0
    e127 = np.zeros((128, 1), np.float32)
    e127[127, 0] = 1.0
    sel = np.zeros((64, 1024), np.float32)
    for b in range(8):
        for j in range(128):
            sel[8 * b + j // 16, 128 * b + j] = 1.0
    return {
        "c_i128f": i128, "c_i128b": i128.astype(bfdt), "c_upper": upper01,
        "c_delta": delta, "c_e127": e127, "c_sel": sel.astype(bfdt),
    }


def shard_inputs(ins):
    W = np.ascontiguousarray(ins["L0_attn_in_proj_w"], np.float32)
    A = np.ascontiguousarray(ins["L0_ffn_A"], np.float32).reshape(1024, 4096)
    Wr = np.ascontiguousarray(ins["L0_ffn_Wr"], np.float32)
    cw = np.ascontiguousarray(ins["L0_attn_conv_w"][:, 0, :], np.float32)
    cb = np.ascontiguousarray(ins["L0_attn_conv_b"], np.float32)
    wout = np.ascontiguousarray(ins["L0_attn_out_proj_w"], np.float32)
    wotile = np.ascontiguousarray(
        wout.reshape(32, 128, 64, 128).transpose(0, 3, 2, 1)
        .reshape(32, 128, 8192))
    bp = np.ascontiguousarray(
        ins["L0_ffn_B"].transpose(0, 2, 1).reshape(1024, 4096), np.float32)
    consts = make_consts()
    maps = []
    for c in range(8):
        b, hh = divmod(c, 2)
        g0 = 4096 * hh
        wcat = np.concatenate([
            W[g0:g0 + 4096],
            W[8192 + g0:8192 + g0 + 4096],
            W[16384 + 512 * hh:16384 + 512 * hh + 512],
            W[17408 + 512 * hh:17408 + 512 * hh + 512],
            A,
            W[18432 + 64 * hh:18432 + 64 * hh + 64],
            Wr,
            np.zeros((128, 4096), np.float32),
        ], axis=0)
        convw_c = np.concatenate([
            cw[g0:g0 + 4096], cw[8192 + 512 * hh:8192 + 512 * hh + 512],
            cw[9216 + 512 * hh:9216 + 512 * hh + 512]], axis=0)
        convb_c = np.concatenate([
            cb[g0:g0 + 4096], cb[8192 + 512 * hh:8192 + 512 * hh + 512],
            cb[9216 + 512 * hh:9216 + 512 * hh + 512]], axis=0).reshape(-1, 1)
        wtile = np.ascontiguousarray(
            wcat.reshape(NSLAB, 128, 32, 128).transpose(0, 3, 2, 1)
            .reshape(NSLAB, 128, 4096))
        m = {
            "x": np.ascontiguousarray(ins["L0_attn_hidden_states"][b],
                                      np.float32),
            "wtile": wtile,
            "convw": np.ascontiguousarray(convw_c),
            "convb": np.ascontiguousarray(convb_c),
            "dtbias": np.ascontiguousarray(
                ins["L0_attn_dt_bias"][64 * hh:64 * hh + 64].reshape(64, 1),
                np.float32),
            "alog": np.ascontiguousarray(
                ins["L0_attn_A_log"][64 * hh:64 * hh + 64].reshape(64, 1),
                np.float32),
            "dvec": np.ascontiguousarray(
                ins["L0_attn_D"][64 * hh:64 * hh + 64].reshape(1, 64),
                np.float32),
            "nw": np.ascontiguousarray(
                ins["L0_attn_norm_weight"][4096 * hh:4096 * hh + 4096]
                .reshape(1, 4096), np.float32),
            "wotile": wotile,
            "bmat": bp,
        }
        m.update(consts)
        maps.append(m)
    return maps


def kernel(L0_attn_hidden_states, L0_attn_in_proj_w, L0_attn_conv_w,
           L0_attn_conv_b, L0_attn_dt_bias, L0_attn_A_log, L0_attn_D,
           L0_attn_norm_weight, L0_attn_out_proj_w, L0_ffn_A, L0_ffn_B,
           L0_ffn_Wr):
    ins = dict(L0_attn_hidden_states=np.asarray(L0_attn_hidden_states),
               L0_attn_in_proj_w=np.asarray(L0_attn_in_proj_w),
               L0_attn_conv_w=np.asarray(L0_attn_conv_w),
               L0_attn_conv_b=np.asarray(L0_attn_conv_b),
               L0_attn_dt_bias=np.asarray(L0_attn_dt_bias),
               L0_attn_A_log=np.asarray(L0_attn_A_log),
               L0_attn_D=np.asarray(L0_attn_D),
               L0_attn_norm_weight=np.asarray(L0_attn_norm_weight),
               L0_attn_out_proj_w=np.asarray(L0_attn_out_proj_w),
               L0_ffn_A=np.asarray(L0_ffn_A),
               L0_ffn_B=np.asarray(L0_ffn_B),
               L0_ffn_Wr=np.asarray(L0_ffn_Wr))
    maps = shard_inputs(ins)
    nc = build_nc()
    res = run_bass_kernel_spmd(nc, maps, core_ids=list(range(8)))
    outp = np.empty((4, SEQ, D), np.float32)
    for c in range(8):
        b, hh = divmod(c, 2)
        outp[b, 2048 * hh:2048 * (hh + 1), :] = res.results[c]["out"]
    return outp


# version 08 (final)
# speedup vs baseline: 1.5552x; 1.0116x over previous
import numpy as np
import ml_dtypes
import concourse.bass as bass
import concourse.mybir as mybir
from concourse.bass_utils import run_bass_kernel_spmd
from concourse.bass_types import DynSlice
from contextlib import ExitStack

F32 = mybir.dt.float32
BF16 = mybir.dt.bfloat16
AF = mybir.ActivationFunctionType
ALU = mybir.AluOpType
AX = mybir.AxisListType
bfdt = ml_dtypes.bfloat16

SEQ, D = 4096, 4096
NH, HP, NG, CH = 64, 64, 4, 128
NCH = SEQ // CH
TOK3 = 2048
EPS = 1e-5
R_GATE, R_HS, R_B, R_C, R_A, R_DTWR, R_PAD = 0, 4096, 8192, 8704, 9216, 10240, 10368
WROWS = 10496
FW = 128
NSLAB = WROWS // FW
CONVCH, NCONVFB = 5120, 40
MAGIC = 987654.0
TSUP, NSUP, TTW, NTT = 2048, 2, 512, 4


def build_nc(dump=()):
    nc = bass.Bass("TRN2", target_bir_lowering=False)

    def inp(name, shape, dt=F32):
        return nc.dram_tensor(name, shape, dt, kind="ExternalInput").ap()

    x_in = inp("x", [SEQ, D])
    wtile = inp("wtile", [NSLAB, 128, 4096])
    convw = inp("convw", [CONVCH, 4])
    convb = inp("convb", [CONVCH, 1])
    dtbias = inp("dtbias", [NH, 1])
    alog = inp("alog", [NH, 1])
    dvec = inp("dvec", [1, NH])
    nw = inp("nw", [1, NH * HP])
    wotile = inp("wotile", [32, 128, 8192])
    bmat = inp("bmat", [1024, 4096])
    c_i128f = inp("c_i128f", [128, 128])
    c_i128b = inp("c_i128b", [128, 128], BF16)
    c_upper = inp("c_upper", [128, 128])
    c_delta = inp("c_delta", [4, 2048])
    c_e127 = inp("c_e127", [128, 1])
    c_sel = inp("c_sel", [64, 1024], BF16)

    out = nc.dram_tensor("out", [TOK3, D], F32, kind="ExternalOutput").ap()

    def scr(name, shape, dt=BF16, **kw):
        return nc.dram_tensor(name, shape, dt, kind="Internal", **kw).ap()

    wT = scr("wT", [NSLAB, 128, 4096])
    woT = scr("woT", [32, 128, 8192])
    bnat = scr("bnat", [1024, 4096])
    xn_tok = scr("xn_tok", [SEQ, D])
    sg_s = scr("sg_s", [4096, SEQ])
    hs_s = scr("hs_s", [4096, SEQ])
    bx_s = scr("bx_s", [512, SEQ])
    cx_s = scr("cx_s", [512, SEQ])
    dt32_s = scr("dt32_s", [NH, SEQ], F32)
    dtb_s = scr("dtb_s", [NH, SEQ])
    hgq = scr("hgq", [2, 128, 8, TSUP])
    x_loc = scr("x_loc", [TOK3, D], F32)
    acsflat = scr("acsflat", [64, 128], F32)
    attn_s = scr("attn_s", [D, TOK3])
    ffn_s = scr("ffn_s", [TOK3, D])
    normed_loc = scr("normed_loc", [SEQ, 4096])
    normed_loc3 = scr("normed_loc3", [TOK3, 8192])
    normed_sh = scr("normed_sh", [SEQ, 8192], addr_space="Shared")
    flag_sh = scr("flag_sh", [4, 64], F32, addr_space="Shared")
    bxcx_sh = scr("bxcx_sh", [2048, SEQ], addr_space="Shared")

    dmap = {"xn_tok": xn_tok, "sg_s": sg_s, "hs_s": hs_s, "bx_s": bx_s,
            "cx_s": cx_s, "dt32_s": dt32_s, "dtb_s": dtb_s, "hgq": hgq,
            "attn_s": attn_s, "ffn_s": ffn_s, "normed_loc": normed_loc,
            "normed_loc3": normed_loc3, "x_loc": x_loc}
    dumps = [(nc.dram_tensor("dump_" + n, list(dmap[n].shape), dmap[n].dtype,
                             kind="ExternalOutput").ap(), dmap[n]) for n in dump]

    es = ExitStack()
    with es:
        def SB(stack):
            def alloc(name, shape, dt=F32):
                return stack.enter_context(nc.sbuf_tensor(name, shape, dt))
            return alloc

        sbp = SB(es)
        i128f = sbp("i128f", [128, 128])
        i128b = sbp("i128b", [128, 128], BF16)
        e127 = sbp("e127", [128, 1])
        selt = sbp("selt", [64, 1024], BF16)
        onesrow = sbp("onesrow", [1, 512])
        acol = sbp("acol", [NH, 1])
        dtbias_t = sbp("dtbias_t", [NH, 1])
        dvec_row = sbp("dvec_row", [1, NH])
        nwrow = sbp("nwrow", [1, NH * HP])
        cw_t = sbp("cw_t", [128, 4 * NCONVFB])
        cb_t = sbp("cb_t", [128, NCONVFB])
        halo = sbp("halo", [128, 3 * NCONVFB], BF16)
        magic_t = sbp("magic_t", [1, 64])
        zrow = sbp("zrow", [1, 64])
        zero64 = sbp("zero64", [NH, 128])
        flagrd = sbp("flagrd", [1, 64])
        epscol = sbp("epscol", [128, 1])
        onecol = sbp("onecol", [128, 1])

        PS = es.enter_context(nc.psum_tensor("PS", [128, 4096], F32))

        def bank(i, n=1):
            return PS[:, 512 * i:512 * (i + n)]

        class Sem:
            def __init__(self, name):
                self.h = es.enter_context(nc.semaphore(name))
                self.n = 0

        qg, qs, qa, qs2 = Sem("qg"), Sem("qs"), Sem("qa"), Sem("qs2")
        qc = Sem("qc")
        qsp = Sem("qsp")
        qsp2 = Sem("qsp2")
        vch, sch, pch, tch = Sem("vch"), Sem("sch"), Sem("pch"), Sem("tch")
        block = es.enter_context(nc.Block())

        def _mk(engsel, chain, selfsync):
            def op(emit, waits=(), inc=None, k=1):
                def body(e):
                    for wt in waits:
                        if wt is None:
                            continue
                    for wt in waits:
                        if wt is None:
                            continue
                        sem, n = wt
                        if n > 0:
                            e.wait_ge(sem.h, n)
                    ins = emit(e)
                    tgt = chain if inc is None else inc
                    if tgt is not None:
                        ins.then_inc(tgt.h, k)
                        tgt.n += k
                        if selfsync and tgt is chain:
                            e.wait_ge(chain.h, chain.n)
                engsel(body)
            return op

        V = _mk(block.vector, vch, True)
        S = _mk(block.scalar, sch, True)
        P = _mk(block.gpsimd, pch, True)
        G = _mk(block.gpsimd, qg, False)

        def mkdma(engsel, sem):
            def op(emit, waits=()):
                def body(e):
                    for wt in waits:
                        if wt is None:
                            continue
                    for wt in waits:
                        if wt is None:
                            continue
                        s2, n = wt
                        if n > 0:
                            e.wait_ge(s2.h, n)
                    emit(e).then_inc(sem.h, 16)
                    sem.n += 16
                engsel(body)
            return op

        GD = mkdma(block.gpsimd, qg)
        QC = mkdma(block.gpsimd, qc)
        Y = mkdma(block.sync, qs)
        Y2 = mkdma(block.sync, qs2)
        SD = mkdma(block.scalar, qa)

        def T(emit, waits=()):
            def body(t):
                for wt in waits:
                    if wt is None:
                        continue
                    sem, n = wt
                    if n > 0:
                        t.wait_ge(sem.h, n)
                emit(t).then_inc(tch.h, 1)
                tch.n += 1
            block.tensor(body)

        def tok(sem):
            return (sem, sem.n)

        for dst, src in [(i128f, c_i128f), (i128b, c_i128b),
                         (e127, c_e127), (selt, c_sel), (acol, alog),
                         (dtbias_t, dtbias), (dvec_row, dvec), (nwrow, nw)]:
            GD(lambda g, dst=dst, src=src: g.dma_start(dst[:, :], src))
        for fb in range(NCONVFB):
            GD(lambda g, fb=fb: g.dma_start(cw_t[:, 4 * fb:4 * fb + 4],
                                            convw[128 * fb:128 * fb + 128, :]))
            GD(lambda g, fb=fb: g.dma_start(cb_t[:, fb:fb + 1],
                                            convb[128 * fb:128 * fb + 128, :]))
        SMALL = tok(qg)

        V(lambda v: v.memset(zrow[:, :], 0.0))
        V(lambda v: v.memset(magic_t[:, :], MAGIC))
        V(lambda v: v.memset(zero64[:, :], 0.0))
        V(lambda v: v.memset(halo[:, :], 0.0))
        V(lambda v: v.memset(onesrow[:, :], 1.0))
        V(lambda v: v.memset(epscol[:, :], EPS))
        V(lambda v: v.memset(onecol[:, :], 1.0))
        CONSTS_V = tok(vch)

        parity = {}

        def _par(engname, engsel, et):
            def body(e):
                r = nc.alloc_register(et, engname + "_par")
                e.load(r, nc.partition_id_tensor.ap())
                e.reg_alu(r, r, 1, ALU.bitwise_and)
                parity[engname] = e.compute_val(r)
            engsel(body)

        _par("g", block.gpsimd, mybir.EngineType.Pool)
        _par("s", block.sync, mybir.EngineType.SP)

        parity_snap = {}

        def par(e, name):
            if name not in parity_snap:
                parity_snap[name] = e.snap(parity[name], min_val=0, max_val=1)
            return parity_snap[name]

        Y(lambda s: s.dma_start(
            flag_sh[DynSlice(par(s, "s"), 1), :], zrow[:, :]),
          waits=[CONSTS_V])
        Y(lambda s: s.dma_start(
            flag_sh[DynSlice(par(s, "s") + 2, 1), :], zrow[:, :]))
        FLAG_CLEARED = tok(qs)

        MAGIC_U32 = int(np.float32(MAGIC).view(np.uint32))

        def emit_spin(rowbase, spinsem, tag):
            def body(s):
                with s.register(tag + "_flv") as flv, \
                     s.register(tag + "_oth") as oth, \
                     s.register(tag + "_cnt") as cnt:
                    s.load(oth, nc.partition_id_tensor.ap())
                    s.reg_alu(oth, oth, 1, ALU.bitwise_and)
                    s.reg_alu(oth, oth, 1, ALU.bitwise_xor)
                    s.reg_alu(oth, oth, rowbase, ALU.add)
                    o_v = s.snap(s.compute_val(oth), min_val=rowbase,
                                 max_val=rowbase + 1)
                    othf = flag_sh[DynSlice(o_v, 1), :]
                    s.reg_mov(cnt, 0)

                    def poll():
                        s.reg_add(cnt, cnt, 16)
                        s.dma_start(flagrd[:, :], othf).then_inc(spinsem.h, 16)
                        s.wait_ge(spinsem.h, s.compute_val(cnt))
                        s.load(flv, flagrd[0:1, 0:1].bitcast(mybir.dt.uint32))

                    poll()
                    with s.While(lambda: s.compute_val(flv) != MAGIC_U32):
                        poll()
            block.sync(body)

        cast_jobs = []
        for i in range(0, NSLAB, 6):
            cast_jobs.append(("wt", lambda g, i=i: g.dma_start(
                wT[i:min(i + 6, NSLAB), :, :],
                wtile[i:min(i + 6, NSLAB), :, :])))
        for i in range(2):
            cast_jobs.append((None, lambda g, i=i: g.dma_start(
                bnat[i * 512:(i + 1) * 512, :],
                bmat[i * 512:(i + 1) * 512, :])))
        for i in range(32):
            cast_jobs.append((None, lambda g, i=i: g.dma_start(
                woT[i, :, :], wotile[i, :, :])))
        wt_cast_tok = []

        def emit_cast():
            if cast_jobs:
                kind, job = cast_jobs.pop(0)
                QC(job)
                if kind == "wt":
                    wt_cast_tok.append(tok(qc))

        S(lambda s: s.activation(acol[:, :], acol[:, :], AF.Exp), waits=[SMALL])
        S(lambda s: s.mul(acol[:, :], acol[:, :], -1.0))


        with ExitStack() as es1:
            sb1 = SB(es1)
            es1a = ExitStack()
            sb1a = SB(es1a)
            xsb = [sb1a(f"xs{i}", [128, D]) for i in range(4)]
            xnt = [es1a.enter_context(nc.sbuf_tensor(f"xnt{i}", [128, D], BF16))
                   for i in range(4)]
            lnstat = sb1a("lnstat", [128, 8])
            lntrash = es1a.enter_context(
                nc.sbuf_tensor("lntrash", [128, D], BF16))
            xn_done_v = []
            xn_spill = []
            QG_X0 = qg.n
            for i in range(32):
                w = []
                if i >= 4:
                    w.append((vch, xn_done_v[i - 4]))
                GD(lambda g, i=i: g.dma_start(xsb[i % 4][:, :],
                                              x_in[128 * i:128 * (i + 1), :]),
                   waits=w)
                XL = tok(qg)
                S(lambda s, i=i: s.activation(lntrash[:, :], xsb[i % 4][:, :],
                                              AF.Square,
                                              accum_out=lnstat[:, 1:2]),
                  waits=[XL])
                SQ = tok(sch)
                V(lambda v, i=i: v.reduce_sum(lnstat[:, 0:1], xsb[i % 4][:, :],
                                              axis=AX.X), waits=[XL])
                V(lambda v: v.tensor_scalar(lnstat[:, 2:3], lnstat[:, 0:1],
                                            1.0 / D, None, ALU.mult))
                V(lambda v: v.tensor_tensor(lnstat[:, 4:5], lnstat[:, 2:3],
                                            lnstat[:, 2:3], ALU.mult))
                V(lambda v: v.tensor_scalar(lnstat[:, 3:4], lnstat[:, 1:2],
                                            1.0 / D, lnstat[:, 4:5],
                                            ALU.mult, ALU.subtract), waits=[SQ])
                S(lambda s: s.activation(lnstat[:, 5:6], lnstat[:, 3:4], AF.Ln,
                                         bias=epscol[:, :]), waits=[tok(vch)])
                S(lambda s: s.activation(lnstat[:, 6:7], lnstat[:, 5:6], AF.Exp,
                                         scale=-0.5))
                w2 = [tok(sch)]
                if i >= 4:
                    w2.append((qg, xn_spill[i - 4]))
                V(lambda v, i=i: v.tensor_scalar(xnt[i % 4][:, :],
                                                 xsb[i % 4][:, :],
                                                 lnstat[:, 2:3], lnstat[:, 6:7],
                                                 ALU.subtract, ALU.mult),
                  waits=w2)
                xn_done_v.append(vch.n)
                GD(lambda g, i=i: g.dma_start(xn_tok[128 * i:128 * (i + 1), :],
                                              xnt[i % 4][:, :]),
                   waits=[(vch, xn_done_v[i])])
                xn_spill.append(qg.n)
                emit_cast()
                emit_cast()
            while cast_jobs:
                emit_cast()
            LN_DONE = tok(qg)
            CASTS = tok(qc)
            WT_DONE = CASTS
            WOT_DONE = CASTS
            es1a.close()

            xnT = [es1.enter_context(nc.sbuf_tensor(f"xnT{i}", [128, TSUP], BF16))
                   for i in range(32)]
            wslab = [es1.enter_context(nc.sbuf_tensor(f"wslab{i}",
                                                      [128, 32 * FW], BF16))
                     for i in range(2)]
            evac_t = [es1.enter_context(nc.sbuf_tensor(f"evac{i}", [128, TTW],
                                                       BF16)) for i in range(2)]
            convo_t = [es1.enter_context(nc.sbuf_tensor(f"convo{i}", [128, TTW],
                                                        BF16)) for i in range(2)]
            convin = es1.enter_context(nc.sbuf_tensor("convin", [128, TTW + 3],
                                                      BF16))
            diag_t = [es1.enter_context(nc.sbuf_tensor(f"diag{i}", [128, 128],
                                                       BF16)) for i in range(4)]
            dtw = [sb1(f"dtw{i}", [64, TTW]) for i in range(2)]
            scf = sb1("scf", [64, TTW])
            sct = sb1("sct", [128, 64])
            mx1, mx2 = sb1("mx1", [128, 8]), sb1("mx2", [128, 8])
            wk1, msk = sb1("wk1", [128, 64]), sb1("msk", [128, 64])
            gex, gsc = sb1("gex", [128, 64]), sb1("gsc", [128, 2])
            g_fm = es1.enter_context(nc.sbuf_tensor("g_fm", [64, TSUP], BF16))
            h_t = es1.enter_context(nc.sbuf_tensor("h_t", [128, TTW], BF16))
            hg_t = es1.enter_context(nc.sbuf_tensor("hg_t", [128, TTW], BF16))
            dt32t = sb1("dt32t", [64, TTW])
            dtbft = es1.enter_context(nc.sbuf_tensor("dtbft", [64, TTW], BF16))

            def fb_role(fb):
                r0 = fb * 128
                if r0 < R_HS:
                    return ("gate", r0 - R_GATE)
                if r0 < R_B:
                    return ("hs", r0 - R_HS)
                if r0 < R_C:
                    return ("b", r0 - R_B)
                if r0 < R_A:
                    return ("c", r0 - R_C)
                if r0 < R_DTWR:
                    return ("a", r0 - R_A)
                if r0 < R_PAD:
                    return ("dtwr", 0)
                return ("pad", 0)

            inproj_bank_free = [None] * 4
            conv_bank_free = [None] * 2
            evac_free = [None] * 2
            convo_free = [None] * 2
            conv_mm_last = [None]
            halo_upd_last = [None]
            hg_spill_last = [None]
            mm_group = 0
            conv_group = 0
            last_super_mm = 0

            for sup in range(NSUP):
                t0 = sup * TSUP
                XNT0 = qs.n
                for dd in range(32):
                    w = [LN_DONE] if dd == 0 else []
                    if sup > 0 and dd == 0:
                        w.append((tch, last_super_mm))
                    Y(lambda s, dd=dd, t0=t0: s.dma_start_transpose(
                        xnT[dd][:, :],
                        xn_tok[t0:t0 + TSUP, 128 * dd:128 * dd + 128]), waits=w)
                XNT_OK = (qs, XNT0 + 16 * 32)

                slab_order = list(range(72)) + [80] + list(range(72, 80))
                slab_loaded = {}
                slab_consumed = {}

                def load_slab(kk, sup=sup):
                    sl = slab_order[kk]
                    w = [wt_cast_tok[sl // 6]] if sup == 0 else []
                    if sup > 0 and kk < 2:
                        w.append((tch, last_super_mm))
                    if kk >= 2:
                        w.append((tch, slab_consumed[kk - 2]))
                    GD(lambda g, sl=sl, kk=kk: g.dma_start(
                        wslab[kk % 2][:, :], wT[sl, :, :]), waits=w)
                    slab_loaded[kk] = qg.n

                load_slab(0)
                load_slab(1)

                for k, sl in enumerate(slab_order):
                    for blkk in [0]:
                        fb = sl
                        role, off = fb_role(fb)
                        if role == "pad":
                            continue
                        for tt in range(NTT):
                            gidx = mm_group
                            mm_group += 1
                            bnk = gidx % 4

                            def mm_emit(t, k=k, blkk=blkk, tt=tt, bnk=bnk):
                                for dd in range(32):
                                    ins = t.matmul(
                                        bank(bnk),
                                        wslab[k % 2][:, FW * dd + 128 * blkk:
                                                     FW * dd + 128 * blkk + 128],
                                        xnT[dd][:, TTW * tt:TTW * (tt + 1)],
                                        start=(dd == 0), stop=(dd == 31))
                                return ins

                            T(mm_emit, waits=[(qg, slab_loaded[k]), XNT_OK,
                                              inproj_bank_free[bnk]])
                            MM_OK = tok(tch)

                            if role == "gate":
                                ev = gidx % 2
                                S(lambda s, bnk=bnk, ev=ev: s.activation(
                                    evac_t[ev][:, :], bank(bnk), AF.Silu),
                                  waits=[MM_OK, evac_free[ev]])
                                inproj_bank_free[bnk] = tok(sch)
                                GD(lambda g, off=off, tt=tt, t0=t0, ev=ev:
                                   g.dma_start(
                                       sg_s[off:off + 128,
                                            t0 + TTW * tt:t0 + TTW * (tt + 1)],
                                       evac_t[ev][:, :]), waits=[tok(sch)])
                                evac_free[ev] = tok(qg)
                            elif role in ("hs", "b", "c"):
                                cfb = ({"hs": 0, "b": 32, "c": 36}[role]
                                       + off // 128)
                                dst = {"hs": hs_s, "b": bx_s, "c": cx_s}[role]
                                V(lambda v, bnk=bnk: v.tensor_copy(
                                    convin[:, 3:3 + TTW], bank(bnk)),
                                  waits=[MM_OK, conv_mm_last[0],
                                         halo_upd_last[0]])
                                inproj_bank_free[bnk] = tok(vch)
                                EVC = tok(vch)
                                V(lambda v, cfb=cfb: v.tensor_copy(
                                    convin[:, 0:3],
                                    halo[:, 3 * cfb:3 * cfb + 3]), waits=[EVC])
                                if tt == 0:
                                    for kk in range(4):
                                        V(lambda v, cfb=cfb, kk=kk:
                                          v.tensor_scalar(
                                              diag_t[kk][:, :], i128b[:, :],
                                              cw_t[:, 4 * cfb + kk:
                                                   4 * cfb + kk + 1],
                                              None, ALU.mult),
                                          waits=[conv_mm_last[0]])
                                CIN = tok(vch)
                                cgidx = conv_group
                                conv_group += 1
                                cbnk = 4 + cgidx % 2

                                def conv_emit(t, cbnk=cbnk):
                                    for kk in range(4):
                                        ins = t.matmul(
                                            bank(cbnk), diag_t[kk][:, :],
                                            convin[:, kk:kk + TTW],
                                            start=(kk == 0), stop=(kk == 3))
                                    return ins

                                T(conv_emit,
                                  waits=[CIN, conv_bank_free[cgidx % 2]])
                                conv_mm_last[0] = tok(tch)
                                CONV_OK = tok(tch)
                                V(lambda v, cfb=cfb: v.tensor_copy(
                                    halo[:, 3 * cfb:3 * cfb + 3],
                                    convin[:, TTW:TTW + 3]), waits=[CONV_OK])
                                halo_upd_last[0] = tok(vch)
                                co = cgidx % 2
                                S(lambda s, cbnk=cbnk, cfb=cfb, co=co:
                                  s.activation(convo_t[co][:, :], bank(cbnk),
                                               AF.Silu,
                                               bias=cb_t[:, cfb:cfb + 1]),
                                  waits=[CONV_OK, convo_free[co]])
                                conv_bank_free[cgidx % 2] = tok(sch)
                                GD(lambda g, dst=dst, off=off, tt=tt, t0=t0,
                                   co=co: g.dma_start(
                                       dst[off:off + 128,
                                           t0 + TTW * tt:t0 + TTW * (tt + 1)],
                                       convo_t[co][:, :]), waits=[tok(sch)])
                                convo_free[co] = tok(qg)
                            elif role == "dtwr":
                                S(lambda s, bnk=bnk: s.activation(
                                    dtw[0][:, :], bank(bnk)[0:64, :], AF.Abs,
                                    bias=dtbias_t[:, :]), waits=[MM_OK])
                                S(lambda s: s.activation(dtw[1][:, :],
                                                         dtw[0][:, :],
                                                         AF.Exp, scale=-1.0))
                                S(lambda s: s.activation(dtw[0][:, :],
                                                         dtw[1][:, :],
                                                         AF.Ln,
                                                         bias=onecol[0:64, :]))
                                S(lambda s, bnk=bnk: s.activation(
                                    dtw[1][:, :], bank(bnk)[0:64, :], AF.Relu,
                                    bias=dtbias_t[:, :]))
                                S(lambda s, bnk=bnk: s.activation(
                                    scf[:, :], bank(bnk)[64:128, :],
                                    AF.Identity))
                                inproj_bank_free[bnk] = tok(sch)
                                V(lambda v: v.tensor_add(dt32t[:, :],
                                                         dtw[0][:, :],
                                                         dtw[1][:, :]),
                                  waits=[tok(sch)])
                                V(lambda v: v.tensor_copy(dtbft[:, :],
                                                          dt32t[:, :]))
                                DTV = tok(vch)
                                GD(lambda g, tt=tt, t0=t0: g.dma_start(
                                    dt32_s[:, t0 + TTW * tt:
                                           t0 + TTW * (tt + 1)], dt32t[:, :]),
                                   waits=[DTV])
                                GD(lambda g, tt=tt, t0=t0: g.dma_start(
                                    dtb_s[:, t0 + TTW * tt:
                                          t0 + TTW * (tt + 1)], dtbft[:, :]),
                                   waits=[DTV])

                                for q in range(4):
                                    T(lambda t, q=q: t.transpose(
                                        bank(6)[:, 0:64],
                                        scf[:, 128 * q:128 * (q + 1)],
                                        i128f[0:64, 0:64]),
                                      waits=[tok(sch), tok(vch)])
                                    S(lambda s: s.activation(
                                        sct[:, :], bank(6)[:, 0:64],
                                        AF.Identity), waits=[tok(tch)])
                                    V(lambda v: v.max(out=mx1[:, :],
                                                      in_=sct[:, :]),
                                      waits=[tok(sch)])
                                    V(lambda v: v.match_replace(
                                        out=wk1[:, :], in_to_replace=mx1[:, :],
                                        in_values=sct[:, :], imm_value=-1e30))
                                    V(lambda v: v.max(out=mx2[:, :],
                                                      in_=wk1[:, :]))
                                    V(lambda v: v.tensor_single_scalar(
                                        msk[:, :], sct[:, :], mx2[:, 7:8],
                                        ALU.is_ge))
                                    V(lambda v: v.tensor_scalar(
                                        gsc[:, 0:1], mx1[:, 0:1], -1.0, None,
                                        ALU.mult))
                                    S(lambda s: s.activation(
                                        gex[:, :], sct[:, :], AF.Exp,
                                        bias=gsc[:, 0:1]), waits=[tok(vch)])
                                    V(lambda v: v.tensor_mul(
                                        gex[:, :], gex[:, :], msk[:, :]),
                                      waits=[tok(sch)])
                                    V(lambda v: v.reduce_sum(
                                        gsc[:, 1:2], gex[:, :], axis=AX.X))
                                    V(lambda v: v.reciprocal(gsc[:, 1:2],
                                                             gsc[:, 1:2]))
                                    V(lambda v: v.tensor_scalar(
                                        gex[:, :], gex[:, :], gsc[:, 1:2],
                                        None, ALU.mult))
                                    T(lambda t: t.transpose(
                                        bank(6)[0:64, 64:192], gex[:, :],
                                        i128f[:, :]),
                                      waits=[tok(vch), tok(sch)])
                                    S(lambda s, q=q, tt=tt: s.activation(
                                        g_fm[0:64, TTW * tt + 128 * q:
                                             TTW * tt + 128 * (q + 1)],
                                        bank(6)[0:64, 64:192], AF.Identity),
                                      waits=[tok(tch)])
                            elif role == "a":
                                bkr = off // 128
                                S(lambda s, bnk=bnk: s.activation(
                                    h_t[:, :], bank(bnk), AF.Identity),
                                  waits=[MM_OK, hg_spill_last[0]])
                                inproj_bank_free[bnk] = tok(sch)
                                H_OK = tok(sch)
                                T(lambda t, bkr=bkr, tt=tt: t.matmul(
                                    bank(6),
                                    selt[:, 128 * bkr:128 * (bkr + 1)],
                                    g_fm[0:64, TTW * tt:TTW * (tt + 1)],
                                    start=True, stop=True),
                                  waits=[H_OK, tok(vch)])
                                V(lambda v: v.tensor_mul(hg_t[:, :], h_t[:, :],
                                                         bank(6)),
                                  waits=[tok(tch)])
                                GD(lambda g, bkr=bkr, tt=tt, sup=sup:
                                   g.dma_start(
                                       hgq[sup, :, bkr,
                                           TTW * tt:TTW * (tt + 1)],
                                       hg_t[:, :]), waits=[tok(vch)])
                                hg_spill_last[0] = tok(qg)
                    slab_consumed[k] = tch.n
                    if k + 2 < len(slab_order):
                        load_slab(k + 2)
                last_super_mm = tch.n
            P1_SPILLS = tok(qg)
            P1_V = tok(vch)
            P1_S = tok(sch)
            P1_T = tok(tch)

        with ExitStack() as es2:
            sb2 = SB(es2)
            hs_tok = [es2.enter_context(nc.sbuf_tensor(f"hs_tok{i}",
                                                       [128, 4096], BF16))
                      for i in range(2)]
            sg_tok = [es2.enter_context(nc.sbuf_tensor(f"sg_tok{i}",
                                                       [128, 4096], BF16))
                      for i in range(2)]
            dttokb = [es2.enter_context(nc.sbuf_tensor(f"dttokb{i}", [128, 64],
                                                       BF16)) for i in range(2)]
            dtfm = [sb2(f"dtfm{i}", [64, 128]) for i in range(2)]
            bfm = [es2.enter_context(nc.sbuf_tensor(f"bfm{i}", [128, 1024],
                                                    BF16)) for i in range(2)]
            cfm = [es2.enter_context(nc.sbuf_tensor(f"cfm{i}", [128, 1024],
                                                    BF16)) for i in range(2)]
            btok = [es2.enter_context(nc.sbuf_tensor(f"btok{i}", [128, 1024],
                                                     BF16)) for i in range(2)]
            arena = [sb2(f"arena{i}", [128, 1024]) for i in range(2)]
            lhsT_a = [sb2(f"lhsTa{i}", [128, 128]) for i in range(2)]
            sstate = sb2("sstate", [128, NH * HP])
            sstate_bf = es2.enter_context(nc.sbuf_tensor("sstate_bf",
                                                         [128, NH * HP], BF16))
            acs_fm = sb2("acs_fm", [64, 128])
            adt_fm = sb2("adt_fm", [64, 128])
            acs_tok = sb2("acs_tok", [128, 64])
            lndt = sb2("lndt", [128, 64])
            c4t = sb2("c4t", [128, 64])
            a_ct = sb2("a_ct", [1, 64])
            d2t = sb2("d2t", [128, 64])
            decayt = es2.enter_context(nc.sbuf_tensor("decayt", [128, 64], BF16))
            sdot = sb2("sdot", [128, 64])
            expat = sb2("expat", [128, 64])
            dtdec = es2.enter_context(nc.sbuf_tensor("dtdec", [128, 64], BF16))
            drest = es2.enter_context(nc.sbuf_tensor("drest", [128, 4096], BF16))
            xhatt = es2.enter_context(nc.sbuf_tensor("xhatt", [128, 4096], BF16))
            gm_m = es2.enter_context(nc.sbuf_tensor("gm_m", [128, 128], BF16))
            esb = [es2.enter_context(nc.sbuf_tensor(f"esb{i}", [128, 512], BF16))
                   for i in range(2)]
            eexp = [es2.enter_context(nc.sbuf_tensor(f"eexp{i}", [128, 512],
                                                     BF16)) for i in range(2)]
            mslab = [es2.enter_context(nc.sbuf_tensor(f"mslab{i}", [128, 512],
                                                      BF16)) for i in range(2)]
            ytmp = sb2("ytmp", [128, 512])
            ybf = es2.enter_context(nc.sbuf_tensor("ybf", [128, 4096], BF16))
            xgt = es2.enter_context(nc.sbuf_tensor("xgt", [128, 4096], BF16))
            nx1 = es2.enter_context(nc.sbuf_tensor("nx1", [128, 4096], BF16))
            nrmt = es2.enter_context(nc.sbuf_tensor("nrmt", [128, 4096], BF16))
            ssqt = sb2("ssqt", [128, 8])
            sqtrash = es2.enter_context(nc.sbuf_tensor("sqtrash", [128, 1024],
                                                       BF16))

            upper = sb2("upper", [128, 128])
            d_rows = es2.enter_context(nc.sbuf_tensor("d_rows", [128, NH],
                                                      BF16))
            nw_rows = es2.enter_context(nc.sbuf_tensor("nw_rows",
                                                       [128, NH * HP], BF16))
            GD(lambda g: g.dma_start(upper[:, :], c_upper))
            T(lambda t: t.matmul(PS[:, 0:NH], onesrow[0:1, 0:128],
                                 dvec_row[0:1, :], start=True, stop=True),
              waits=[SMALL, CONSTS_V, P1_T])
            V(lambda v: v.tensor_copy(d_rows[:, :], PS[:, 0:NH]),
              waits=[tok(tch), P1_V])
            for j in range(8):
                T(lambda t, j=j: t.matmul(bank(1), onesrow[0:1, 0:128],
                                          nwrow[0:1, 512 * j:512 * (j + 1)],
                                          start=True, stop=True),
                  waits=[tok(vch)])
                V(lambda v, j=j: v.tensor_copy(
                    nw_rows[:, 512 * j:512 * (j + 1)], bank(1)),
                  waits=[tok(tch)])
            V(lambda v: v.memset(sstate[:, :], 0.0))
            V(lambda v: v.memset(sstate_bf[:, :], 0.0))
            V(lambda v: v.memset(arena[0][:, :], 0.0))
            V(lambda v: v.memset(arena[1][:, :], 0.0))
            V(lambda v: v.memset(lhsT_a[0][:, :], 0.0))
            V(lambda v: v.memset(lhsT_a[1][:, :], 0.0))
            AR_Z = tok(vch)
            GD(lambda g: g.dma_start(arena[0][0:4, :], c_delta[:, 0:1024]),
               waits=[AR_Z])
            GD(lambda g: g.dma_start(arena[1][0:4, :], c_delta[:, 0:1024]))
            GD(lambda g: g.dma_start(lhsT_a[0][4:5, 0:128],
                                     onesrow[0:1, 0:128]))
            GD(lambda g: g.dma_start(lhsT_a[1][4:5, 0:128],
                                     onesrow[0:1, 0:128]))
            ARENA_INIT = tok(qg)

            GD(lambda g: g.dma_start(
                bxcx_sh[DynSlice(par(g, "g") * 512, 512), :], bx_s),
               waits=[P1_SPILLS])
            GD(lambda g: g.dma_start(
                bxcx_sh[DynSlice(1024 + par(g, "g") * 512, 512), :], cx_s))
            BXCX_W = tok(qg)
            GD(lambda g: g.dma_start(
                flag_sh[DynSlice(par(g, "g") + 2, 1), :], magic_t[:, :]),
               waits=[BXCX_W, FLAG_CLEARED])
            emit_spin(2, qsp2, "sp2")

            arena_free = [None, None]
            ebank_free = [None, None]
            mslab_free = [None, None]
            mslab_ok = [None, None]
            ydbank_free = [None, None]
            eexp_s = [None, None]
            yobank_free = [None, None]
            stbank_free = [None, None]
            sbf_ok = [None] * 8
            chunk_end_v = {}
            chunk_end_t = {}

            for c in range(NCH):
                p2 = c % 2
                ct0 = c * 128
                cw = [BXCX_W] if c == 0 else []
                if c >= 2:
                    cw = [(vch, chunk_end_v[c - 2]), (tch, chunk_end_t[c - 2])]
                Y(lambda s: s.dma_start_transpose(hs_tok[p2][:, :],
                                                  hs_s[:, ct0:ct0 + 128]),
                  waits=cw)
                Y(lambda s: s.dma_start_transpose(sg_tok[p2][:, :],
                                                  sg_s[:, ct0:ct0 + 128]))
                Y(lambda s: s.dma_start_transpose(dttokb[p2][:, :],
                                                  dtb_s[:, ct0:ct0 + 128]))
                Y(lambda s: s.dma_start_transpose(btok[p2][:, :],
                                                  bxcx_sh[0:1024,
                                                          ct0:ct0 + 128]))
                Y(lambda s: s.dma_start(dtfm[p2][:, :],
                                        dt32_s[:, ct0:ct0 + 128]))
                Y(lambda s: s.dma_start(
                    bfm[p2][:, :].rearrange("p (g t) -> p g t", g=8),
                    bxcx_sh[0:1024, ct0:ct0 + 128].rearrange(
                        "(g p) t -> p g t", p=128)))
                Y(lambda s: s.dma_start(
                    cfm[p2][:, :].rearrange("p (g t) -> p g t", g=8),
                    bxcx_sh[1024:2048, ct0:ct0 + 128].rearrange(
                        "(g p) t -> p g t", p=128)))
                LD = tok(qs)

                V(lambda v: v.tensor_scalar(adt_fm[:, :], dtfm[p2][:, :],
                                            acol[:, :], None, ALU.mult),
                  waits=[LD])
                V(lambda v: v.tensor_tensor_scan(acs_fm[:, :], adt_fm[:, :],
                                                 zero64[:, :], 0.0,
                                                 ALU.add, ALU.add))
                ACSFM_V = tok(vch)
                GD(lambda g: g.dma_start(acsflat, acs_fm[:, :]),
                   waits=[ACSFM_V, arena_free[0], arena_free[1]])
                ACSFLAT_W = tok(qg)
                T(lambda t: t.transpose(bank(1)[:, 0:64], acs_fm[:, :],
                                        i128f[0:64, 0:64]),
                  waits=[ACSFM_V, ydbank_free[1]])
                S(lambda s: s.activation(acs_tok[:, :], bank(1)[:, 0:64],
                                         AF.Identity), waits=[tok(tch)])
                ACS_S = tok(sch)
                S(lambda s: s.activation(lndt[:, :], dttokb[p2][:, :], AF.Ln),
                  waits=[LD])
                V(lambda v: v.tensor_sub(c4t[:, :], acs_tok[:, :],
                                         lndt[:, :]), waits=[tok(sch)])
                C4_V = tok(vch)
                T(lambda t: t.matmul(bank(1)[0:1, 64:128], e127[:, :],
                                     acs_tok[:, :], start=True, stop=True),
                  waits=[ACS_S])
                S(lambda s: s.activation(a_ct[:, :], bank(1)[0:1, 64:128],
                                         AF.Identity), waits=[tok(tch)])
                T(lambda t: t.matmul(bank(1)[:, 128:192], onesrow[0:1, 0:128],
                                     a_ct[0:1, :], start=True, stop=True),
                  waits=[tok(sch)])
                V(lambda v: v.tensor_sub(d2t[:, :], bank(1)[:, 128:192],
                                         acs_tok[:, :]), waits=[tok(tch)])
                D2_V = tok(vch)
                S(lambda s: s.activation(decayt[:, :], d2t[:, :], AF.Exp),
                  waits=[D2_V])
                S(lambda s: s.activation(sdot[:, :], acs_tok[:, :], AF.Exp))
                S(lambda s: s.activation(expat[:, :], bank(1)[:, 128:192],
                                         AF.Exp))
                EXPA_S = tok(sch)
                V(lambda v: v.tensor_mul(dtdec[:, :], dttokb[p2][:, :],
                                         decayt[:, :]), waits=[tok(sch)])
                V(lambda v: v.tensor_tensor(
                    drest[:, :].rearrange("t (h p) -> t h p", p=HP),
                    hs_tok[p2][:, :].rearrange("t (h p) -> t h p", p=HP),
                    d_rows[:, :].unsqueeze(2).broadcast_to([128, NH, HP]),
                    ALU.mult))
                V(lambda v: v.tensor_tensor(
                    xhatt[:, :].rearrange("t (h p) -> t h p", p=HP),
                    hs_tok[p2][:, :].rearrange("t (h p) -> t h p", p=HP),
                    dtdec[:, :].unsqueeze(2).broadcast_to([128, NH, HP]),
                    ALU.mult))
                XHAT_V = tok(vch)

                for sg in range(8):
                    sgp = sg % 2
                    ydb = bank(0 + sgp)
                    yob = bank(2 + sgp)
                    stb = bank(4 + sgp)

                    def gview(ap4096):
                        return ap4096.rearrange(
                            "t (k g2 p) -> t g2 k p", g2=8, p=HP)[:, sg, :, :]

                    T(lambda t, sg=sg: t.matmul(
                        yob[:, 0:128],
                        bfm[p2][:, 128 * sg:128 * (sg + 1)],
                        cfm[p2][:, 128 * sg:128 * (sg + 1)],
                        start=True, stop=True),
                      waits=[LD, yobank_free[sgp]])
                    V(lambda v, sg=sg: v.tensor_tensor(
                        gm_m[:, :], yob[:, 0:128],
                        upper[:, :], ALU.mult), waits=[tok(tch), tok(pch)])
                    GM_V = tok(vch)

                    GD(lambda g, sg=sg: g.dma_start(
                        arena[sgp][4:5, :].rearrange("o (k t) -> o k t", k=8),
                        acsflat.rearrange("(k g2) t -> g2 k t", g2=8)[
                            sg:sg + 1, :, :].squeeze(0).unsqueeze(0)),
                       waits=[ACSFLAT_W, ARENA_INIT, arena_free[sgp]])
                    AR_OK = tok(qg)

                    for si in range(2):
                        j = 2 * sg + si
                        T(lambda t, sg=sg, si=si: t.transpose(
                            stb[0:4, 384:512],
                            c4t[:, :].rearrange("t (k g2) -> t g2 k", g2=8)[
                                :, sg, 4 * si:4 * si + 4],
                            i128f[:, :]),
                          waits=[C4_V, EXPA_S, D2_V, tok(vch),
                                 stbank_free[sgp]])
                        V(lambda v: v.tensor_copy(
                            lhsT_a[j % 2][0:4, :], stb[0:4, 384:512]),
                          waits=[tok(tch)])
                        LH_V = tok(vch)
                        T(lambda t, j=j, sgp=sgp, si=si: t.matmul(
                            bank(6 + j % 2), lhsT_a[j % 2][:, :],
                            arena[sgp][:, 512 * si:512 * (si + 1)],
                            start=True, stop=True),
                          waits=[LH_V, AR_OK, ebank_free[j % 2]])
                        EMM = tok(tch)
                        if si == 1:
                            arena_free[sgp] = EMM
                        V(lambda v, j=j: v.tensor_scalar(
                            esb[j % 2][:, :], bank(6 + j % 2), 30.0, None,
                            ALU.min), waits=[EMM, eexp_s[j % 2]])
                        ebank_free[j % 2] = tok(vch)
                        S(lambda s, j=j: s.activation(
                            eexp[j % 2][:, :], esb[j % 2][:, :], AF.Exp),
                          waits=[tok(vch), tok(pch)])
                        eexp_s[j % 2] = tok(sch)
                        P(lambda p, j=j, si=si: p.tensor_tensor(
                            mslab[si][:, :].rearrange("s (a l) -> s a l", a=4),
                            eexp[j % 2][:, :].rearrange("s (a l) -> s a l",
                                                        a=4),
                            gm_m[:, :].unsqueeze(1)
                            .broadcast_to([128, 4, 128]), ALU.mult),
                          waits=[tok(sch), GM_V, mslab_free[si]])
                        mslab_ok[si] = tok(pch)

                    T(lambda t, sg=sg: t.matmul(
                        ydb, i128b[:, :], gview(drest[:, :]),
                        start=True, stop=True),
                      waits=[XHAT_V, ydbank_free[sgp], EXPA_S, ACS_S])

                    def ydiag_emit(t, sg=sg):
                        for k in range(8):
                            h = sg + 8 * k
                            ins = t.matmul(
                                ydb[:, 64 * k:64 * (k + 1)],
                                mslab[k // 4][:, 128 * (k % 4):
                                              128 * (k % 4 + 1)],
                                hs_tok[p2][:, 64 * h:64 * (h + 1)],
                                start=False, stop=True)
                        return ins
                    T(ydiag_emit, waits=[mslab_ok[0], mslab_ok[1]])
                    YD = tok(tch)
                    mslab_free[0] = YD
                    mslab_free[1] = YD

                    T(lambda t, sg=sg: t.matmul(
                        yob, cfm[p2][:, 128 * sg:128 * (sg + 1)],
                        gview(sstate_bf[:, :]),
                        start=True, stop=True),
                      waits=[LD, yobank_free[sgp], sbf_ok[sg], GM_V])
                    YO = tok(tch)

                    T(lambda t, sg=sg: t.matmul(
                        stb, btok[p2][:, 128 * sg:128 * (sg + 1)],
                        gview(xhatt[:, :]),
                        start=True, stop=True),
                      waits=[XHAT_V, LD, tok(vch), stbank_free[sgp]])
                    ST = tok(tch)

                    V(lambda v, sg=sg: v.tensor_tensor(
                        ytmp[:, :].rearrange("t (a p) -> t a p", p=HP),
                        yob.rearrange("t (a p) -> t a p", p=HP),
                        sdot[:, :].rearrange("t (k g2) -> t g2 k", g2=8)[
                            :, sg, :].unsqueeze(2)
                        .broadcast_to([128, 8, HP]), ALU.mult), waits=[YO])
                    yobank_free[sgp] = tok(vch)
                    V(lambda v, sg=sg: v.tensor_tensor(
                        gview(ybf[:, :]),
                        ytmp[:, :].rearrange("t (a p) -> t a p", p=HP),
                        ydb.rearrange("t (a p) -> t a p", p=HP),
                        ALU.add), waits=[YD])
                    ydbank_free[sgp] = tok(vch)
                    V(lambda v, sg=sg: v.tensor_tensor(
                        gview(sstate[:, :]), gview(sstate[:, :]),
                        expat[:, :].rearrange("t (k g2) -> t g2 k", g2=8)[
                            :, sg, :].unsqueeze(2)
                        .broadcast_to([128, 8, HP]), ALU.mult),
                      waits=[EXPA_S])
                    V(lambda v, sg=sg: v.tensor_tensor(
                        gview(sstate[:, :]), gview(sstate[:, :]),
                        stb.rearrange("n (a p) -> n a p", p=HP),
                        ALU.add), waits=[ST])
                    stbank_free[sgp] = tok(vch)
                    V(lambda v, sg=sg: v.tensor_copy(
                        gview(sstate_bf[:, :]), gview(sstate[:, :])))
                    sbf_ok[sg] = tok(vch)

                V(lambda v: v.tensor_mul(xgt[:, :], ybf[:, :],
                                         sg_tok[p2][:, :]))
                XG_V = tok(vch)
                for g4 in range(4):
                    S(lambda s, g4=g4: s.activation(
                        sqtrash[:, :], xgt[:, 1024 * g4:1024 * (g4 + 1)],
                        AF.Square, accum_out=ssqt[:, g4:g4 + 1]),
                      waits=[XG_V])
                S(lambda s: s.activation(ssqt[:, 4:8], ssqt[:, 0:4], AF.Ln,
                                         scale=1.0 / 1024.0,
                                         bias=epscol[:, :]))
                S(lambda s: s.activation(ssqt[:, 0:4], ssqt[:, 4:8], AF.Exp,
                                         scale=-0.5))
                V(lambda v: v.tensor_tensor(
                    nx1[:, :].rearrange("t (a f) -> t a f", a=4),
                    xgt[:, :].rearrange("t (a f) -> t a f", a=4),
                    ssqt[:, 0:4].unsqueeze(2).broadcast_to([128, 4, 1024]),
                    ALU.mult), waits=[tok(sch)])
                V(lambda v: v.tensor_mul(nrmt[:, :], nx1[:, :], nw_rows[:, :]))
                NRM_V = tok(vch)
                GD(lambda g, c=c: g.dma_start(
                    normed_loc[128 * c:128 * (c + 1), :],
                    nrmt[:, :]), waits=[NRM_V])
                chunk_end_v[c] = vch.n
                chunk_end_t[c] = tch.n
            P2_SPILLS = tok(qg)
            P2_T = tok(tch)
            P2_V = tok(vch)
            P2_P = tok(pch)
            P2_S2 = tok(sch)

        GD(lambda g: g.dma_start(
            normed_sh[:, DynSlice(par(g, "g") * 4096, 4096)],
            normed_loc), waits=[P2_SPILLS])
        NORMED_SH_W = tok(qg)
        GD(lambda g: g.dma_start(flag_sh[parity["g"], :].unsqueeze(0),
                                 magic_t[:, :]),
           waits=[NORMED_SH_W, FLAG_CLEARED])

        with ExitStack() as esf:
            hgt3 = esf.enter_context(nc.sbuf_tensor("hgt3", [128, 8, 2048],
                                                    BF16))
            bds = [esf.enter_context(nc.sbuf_tensor(f"bds{i}", [128, 8192],
                                                    BF16)) for i in range(2)]
            ff_t = [esf.enter_context(nc.sbuf_tensor(f"ff_t{i}", [128, 1024],
                                                     BF16)) for i in range(2)]
            Y2(lambda s: s.dma_start(
                hgt3[:, :, :],
                hgq[DynSlice(par(s, "s"), 1), :, :, :].squeeze(0)),
               waits=[P2_SPILLS, P2_T, P2_V, P2_P, P2_S2])
            Y2(lambda s: s.dma_start(
                x_loc, x_in[DynSlice(par(s, "s") * 2048, 2048), :]))
            HGT_OK = tok(qs2)

            bnr = bnat.rearrange("(kr p) c -> p kr c", p=128)
            bds_tok = [None] * 4
            ffgrp_end = {}

            def load_bds(ds):
                w = ([CASTS, P2_SPILLS, P2_T, P2_V, P2_P]
                     if ds < 2 else [(tch, ffgrp_end[ds - 2])])
                SD(lambda s, ds=ds: s.dma_start(
                    bds[ds % 2][:, :].rearrange("p (kr c) -> p kr c", c=1024),
                    bnr[:, :, 1024 * ds:1024 * (ds + 1)]), waits=w)
                bds_tok[ds] = tok(qa)

            load_bds(0)
            load_bds(1)
            ffbank_free = [None, None]
            ff_free = [None, None]
            for ds in range(4):
                for tc in range(16):
                    idx = ds * 16 + tc
                    pb = 4 + 2 * (idx % 2)

                    def ffb_emit(t, ds=ds, tc=tc, pb=pb):
                        for kr in range(8):
                            for hf in range(2):
                                ins = t.matmul(
                                    bank(pb + hf),
                                    hgt3[:, kr, 128 * tc:128 * (tc + 1)],
                                    bds[ds % 2][:, 1024 * kr + 512 * hf:
                                                1024 * kr + 512 * (hf + 1)],
                                    start=(kr == 0), stop=(kr == 7))
                        return ins
                    wf = [HGT_OK, bds_tok[ds], ffbank_free[idx % 2]]
                    if idx == 0:
                        wf.append(P2_V)
                    T(ffb_emit, waits=wf)
                    FMM = tok(tch)
                    S(lambda s, pb=pb, idx=idx: s.activation(
                        ff_t[idx % 2][:, :], bank(pb, 2), AF.Identity,
                        scale=2.0), waits=[FMM, ff_free[idx % 2]])
                    ffbank_free[idx % 2] = tok(sch)
                    SD(lambda s, ds=ds, tc=tc, idx=idx: s.dma_start(
                        ffn_s[128 * tc:128 * (tc + 1),
                              1024 * ds:1024 * (ds + 1)],
                        ff_t[idx % 2][:, :]))
                    ff_free[idx % 2] = tok(qa)
                ffgrp_end[ds] = tch.n
                if ds + 2 < 4:
                    load_bds(ds + 2)
            FFN_SPILLS = tok(qa)
            FFN_T = tok(tch)

        emit_spin(0, qsp, "sp1")
        Y2(lambda s: s.dma_start(
            normed_loc3[0:1024, :],
            normed_sh[DynSlice(par(s, "s") * 2048, 1024), :]),
           waits=[NORMED_SH_W])
        BULK_H0 = tok(qs2)
        Y2(lambda s: s.dma_start(
            normed_loc3[1024:2048, :],
            normed_sh[DynSlice(1024 + par(s, "s") * 2048, 1024), :]))
        BULK_OK = tok(qs2)

        with ExitStack() as es3:
            sb3 = SB(es3)
            nrm3 = [es3.enter_context(nc.sbuf_tensor(f"nrm3_{i}", [128, 1024],
                                                     BF16)) for i in range(64)]
            es3w = ExitStack()
            wp3 = [es3w.enter_context(nc.sbuf_tensor(f"wp3_{i}", [128, 8192],
                                                     BF16)) for i in range(2)]
            at_t = [es3w.enter_context(nc.sbuf_tensor(f"at_t{i}", [128, 1024],
                                                      BF16)) for i in range(2)]
            p3bank_free = [None, None]
            at_free = [None, None]
            wp3_used = []
            p3_tile_done = []
            for t3 in range(2):
                NRM0 = qs2.n
                for ic in range(64):
                    w = []
                    if t3 == 0 and ic == 0:
                        w = [BULK_H0, FFN_T, FFN_SPILLS, P2_V, P2_P]
                    if t3 > 0 and ic == 0:
                        w = [(tch, p3_tile_done[t3 - 1]), BULK_OK]
                    Y2(lambda s, t3=t3, ic=ic: s.dma_start_transpose(
                        nrm3[ic][:, :],
                        normed_loc3[1024 * t3:1024 * (t3 + 1),
                                    128 * ic:128 * (ic + 1)]), waits=w)
                NRM_OK = (qs2, NRM0 + 16 * 64)
                for og in range(32):
                    gi = t3 * 32 + og
                    w = [WOT_DONE, FFN_T, FFN_SPILLS] if gi == 0 else []
                    if gi >= 2:
                        w.append((tch, wp3_used[gi - 2]))
                    GD(lambda g, og=og, gi=gi: g.dma_start(
                        wp3[gi % 2][:, :], woT[og, :, :]), waits=w)
                    WP3 = tok(qg)
                    for half in range(2):
                        bnk2 = 2 * (gi % 2) + half

                        def p3_emit(t, gi=gi, half=half, bnk2=bnk2):
                            for ic in range(64):
                                ins = t.matmul(
                                    bank(bnk2),
                                    wp3[gi % 2][:, 128 * ic:128 * (ic + 1)],
                                    nrm3[ic][:, 512 * half:512 * (half + 1)],
                                    start=(ic == 0), stop=(ic == 63))
                            return ins
                        T(p3_emit,
                          waits=([WP3, NRM_OK, p3bank_free[gi % 2]]
                                 if half == 0 else []))
                    wp3_used.append(tch.n)
                    PMM = tok(tch)
                    S(lambda s, gi=gi: s.activation(
                        at_t[gi % 2][:, :], bank(2 * (gi % 2), 2),
                        AF.Identity), waits=[PMM, at_free[gi % 2]])
                    p3bank_free[gi % 2] = tok(sch)
                    GD(lambda g, og=og, t3=t3, gi=gi: g.dma_start(
                        attn_s[128 * og:128 * (og + 1),
                               1024 * t3:1024 * (t3 + 1)],
                        at_t[gi % 2][:, :]), waits=[tok(sch)])
                    at_free[gi % 2] = tok(qg)
                p3_tile_done.append(tch.n)
            P3_SPILLS = tok(qg)
            P3_T = tok(tch)
            P3_S = tok(sch)
            es3w.close()

            x4 = [sb3(f"x4_{i}", [128, 1024]) for i in range(4)]
            o4 = [sb3(f"o4_{i}", [128, 1024]) for i in range(4)]
            at4 = [es3.enter_context(nc.sbuf_tensor(f"at4_{i}", [128, 1024],
                                                    BF16)) for i in range(4)]
            ff4 = [es3.enter_context(nc.sbuf_tensor(f"ff4_{i}", [128, 1024],
                                                    BF16)) for i in range(4)]
            p4_done = []
            p4_spill = []
            for ii in range(64):
                tc, dch = divmod(ii, 4)
                q4 = ii % 4
                w = []
                if ii == 0:
                    w = [P3_T, P3_SPILLS, P3_S, FFN_SPILLS]
                if ii >= 4:
                    w = [(vch, p4_done[ii - 4])]
                GD(lambda g, tc=tc, dch=dch, q4=q4: g.dma_start(
                    x4[q4][:, :],
                    x_loc[128 * tc:128 * (tc + 1),
                          1024 * dch:1024 * (dch + 1)]), waits=w)
                X4L = tok(qg)
                Y2(lambda s, tc=tc, dch=dch, q4=q4: s.dma_start_transpose(
                    at4[q4][:, :],
                    attn_s[1024 * dch:1024 * (dch + 1),
                           128 * tc:128 * (tc + 1)]), waits=w)
                A4L = tok(qs2)
                SD(lambda s, tc=tc, dch=dch, q4=q4: s.dma_start(
                    ff4[q4][:, :],
                    ffn_s[128 * tc:128 * (tc + 1),
                          1024 * dch:1024 * (dch + 1)]), waits=w)
                F4L = tok(qa)
                w5 = [X4L, A4L, F4L]
                if ii >= 4:
                    w5.append((qa, p4_spill[ii - 4]))
                V(lambda v, q4=q4: v.tensor_add(o4[q4][:, :], x4[q4][:, :],
                                                at4[q4][:, :]), waits=w5)
                V(lambda v, q4=q4: v.tensor_add(o4[q4][:, :], o4[q4][:, :],
                                                ff4[q4][:, :]))
                p4_done.append(vch.n)
                SD(lambda s, tc=tc, dch=dch, q4=q4: s.dma_start(
                    out[128 * tc:128 * (tc + 1),
                        1024 * dch:1024 * (dch + 1)], o4[q4][:, :]),
                   waits=[tok(vch)])
                p4_spill.append(qa.n)
            OUT_DONE = tok(qa)

        for dap, src in dumps:
            GD(lambda g, dap=dap, src=src: g.dma_start(dap, src),
               waits=[OUT_DONE])
        DUMP_DONE = tok(qg)

        @block.sync
        def _(s):
            s.wait_ge(qa.h, qa.n)
            s.wait_ge(qg.h, DUMP_DONE[1])
            s.wait_ge(qs2.h, qs2.n)
    return nc


def make_consts():
    i128 = np.eye(128, dtype=np.float32)
    upper01 = (np.arange(128)[None, :] >= np.arange(128)[:, None]).astype(
        np.float32)
    delta = np.zeros((4, 2048), np.float32)
    for sl in range(4):
        for k in range(4):
            delta[k, 512 * sl + 128 * k:512 * sl + 128 * (k + 1)] = -1.0
    e127 = np.zeros((128, 1), np.float32)
    e127[127, 0] = 1.0
    sel = np.zeros((64, 1024), np.float32)
    for b in range(8):
        for j in range(128):
            sel[8 * b + j // 16, 128 * b + j] = 1.0
    return {
        "c_i128f": i128, "c_i128b": i128.astype(bfdt), "c_upper": upper01,
        "c_delta": delta, "c_e127": e127, "c_sel": sel.astype(bfdt),
    }


def shard_inputs(ins):
    W = np.ascontiguousarray(ins["L0_attn_in_proj_w"], np.float32)
    A = np.ascontiguousarray(ins["L0_ffn_A"], np.float32).reshape(1024, 4096)
    Wr = np.ascontiguousarray(ins["L0_ffn_Wr"], np.float32)
    cw = np.ascontiguousarray(ins["L0_attn_conv_w"][:, 0, :], np.float32)
    cb = np.ascontiguousarray(ins["L0_attn_conv_b"], np.float32)
    wout = np.ascontiguousarray(ins["L0_attn_out_proj_w"], np.float32)
    wotile = np.ascontiguousarray(
        wout.reshape(32, 128, 64, 128).transpose(0, 3, 2, 1)
        .reshape(32, 128, 8192))
    bp = np.ascontiguousarray(
        ins["L0_ffn_B"].transpose(0, 2, 1).reshape(1024, 4096), np.float32)
    consts = make_consts()
    maps = []
    for c in range(8):
        b, hh = divmod(c, 2)
        g0 = 4096 * hh
        wcat = np.concatenate([
            W[g0:g0 + 4096],
            W[8192 + g0:8192 + g0 + 4096],
            W[16384 + 512 * hh:16384 + 512 * hh + 512],
            W[17408 + 512 * hh:17408 + 512 * hh + 512],
            A,
            W[18432 + 64 * hh:18432 + 64 * hh + 64],
            Wr,
            np.zeros((128, 4096), np.float32),
        ], axis=0)
        convw_c = np.concatenate([
            cw[g0:g0 + 4096], cw[8192 + 512 * hh:8192 + 512 * hh + 512],
            cw[9216 + 512 * hh:9216 + 512 * hh + 512]], axis=0)
        convb_c = np.concatenate([
            cb[g0:g0 + 4096], cb[8192 + 512 * hh:8192 + 512 * hh + 512],
            cb[9216 + 512 * hh:9216 + 512 * hh + 512]], axis=0).reshape(-1, 1)
        wtile = np.ascontiguousarray(
            wcat.reshape(NSLAB, 128, 32, 128).transpose(0, 3, 2, 1)
            .reshape(NSLAB, 128, 4096))
        m = {
            "x": np.ascontiguousarray(ins["L0_attn_hidden_states"][b],
                                      np.float32),
            "wtile": wtile,
            "convw": np.ascontiguousarray(convw_c),
            "convb": np.ascontiguousarray(convb_c),
            "dtbias": np.ascontiguousarray(
                ins["L0_attn_dt_bias"][64 * hh:64 * hh + 64].reshape(64, 1),
                np.float32),
            "alog": np.ascontiguousarray(
                ins["L0_attn_A_log"][64 * hh:64 * hh + 64].reshape(64, 1),
                np.float32),
            "dvec": np.ascontiguousarray(
                ins["L0_attn_D"][64 * hh:64 * hh + 64].reshape(1, 64),
                np.float32),
            "nw": np.ascontiguousarray(
                ins["L0_attn_norm_weight"][4096 * hh:4096 * hh + 4096]
                .reshape(1, 4096), np.float32),
            "wotile": wotile,
            "bmat": bp,
        }
        m.update(consts)
        maps.append(m)
    return maps


def kernel(L0_attn_hidden_states, L0_attn_in_proj_w, L0_attn_conv_w,
           L0_attn_conv_b, L0_attn_dt_bias, L0_attn_A_log, L0_attn_D,
           L0_attn_norm_weight, L0_attn_out_proj_w, L0_ffn_A, L0_ffn_B,
           L0_ffn_Wr):
    ins = dict(L0_attn_hidden_states=np.asarray(L0_attn_hidden_states),
               L0_attn_in_proj_w=np.asarray(L0_attn_in_proj_w),
               L0_attn_conv_w=np.asarray(L0_attn_conv_w),
               L0_attn_conv_b=np.asarray(L0_attn_conv_b),
               L0_attn_dt_bias=np.asarray(L0_attn_dt_bias),
               L0_attn_A_log=np.asarray(L0_attn_A_log),
               L0_attn_D=np.asarray(L0_attn_D),
               L0_attn_norm_weight=np.asarray(L0_attn_norm_weight),
               L0_attn_out_proj_w=np.asarray(L0_attn_out_proj_w),
               L0_ffn_A=np.asarray(L0_ffn_A),
               L0_ffn_B=np.asarray(L0_ffn_B),
               L0_ffn_Wr=np.asarray(L0_ffn_Wr))
    maps = shard_inputs(ins)
    nc = build_nc()
    res = run_bass_kernel_spmd(nc, maps, core_ids=list(range(8)))
    outp = np.empty((4, SEQ, D), np.float32)
    for c in range(8):
        b, hh = divmod(c, 2)
        outp[b, 2048 * hh:2048 * (hh + 1), :] = res.results[c]["out"]
    return outp
